# version 24 (final)
# baseline (speedup 1.0000x reference)
import numpy as np
import ml_dtypes
import concourse.bass as bass
import concourse.mybir as mybir
from concourse.bass_utils import run_bass_kernel_spmd

F32 = mybir.dt.float32
BF16 = mybir.dt.bfloat16
FP8 = mybir.dt.float8e4
AF = mybir.ActivationFunctionType
ALU = mybir.AluOpType
AX = mybir.AxisListType

S, D = 2048, 2048
TG = 512
NG = S // TG
LC = 256
NCH = TG // LC
NH, HP, NST = 64, 64, 128
GATE = 4096
HBCF = 4352
PROJ = 8512
NTT = TG // 128
NFT = D // 128

ENGINES = ("sync", "tensor", "vector", "scalar", "gpsimd")


class Prog:

    def __init__(self):
        self.ops = {e: [] for e in ENGINES}
        self.writers = {}
        self.readers = {}
        self.stream_count = {}
        self.stream_last = {}

    @staticmethod
    def _ov(lo, hi, lo2, hi2):
        return lo < hi2 and lo2 < hi

    def emit(self, eng, fn, reads=(), writes=(), dma=False, force_inc=False, tag="",
             stream=None):
        if dma:
            stream = stream or (eng + "_misc")
        idx = len(self.ops[eng])
        deps = set()
        for (sp, lo, hi) in reads:
            for (lo2, hi2, src) in self.writers.get(sp, ()):
                if self._ov(lo, hi, lo2, hi2):
                    deps.add(src)
        for (sp, lo, hi) in writes:
            for (lo2, hi2, src) in self.writers.get(sp, ()):
                if self._ov(lo, hi, lo2, hi2):
                    deps.add(src)
            for (lo2, hi2, src) in self.readers.get(sp, ()):
                if self._ov(lo, hi, lo2, hi2):
                    deps.add(src)
        me = (eng, idx)
        deps.discard(me)
        if dma:
            prev = self.stream_last.get(stream)
            if prev is not None:
                deps.add(prev)
        snap = dict(self.stream_count)
        self.ops[eng].append(dict(eng=eng, idx=idx, fn=fn, deps=deps, dma=dma,
                                  tag=tag, needs_inc=force_inc, stream=stream,
                                  snap=snap))
        if dma:
            self.stream_count[stream] = self.stream_count.get(stream, 0) + 1
            self.stream_last[stream] = me
        for (sp, lo, hi) in writes:
            self.writers[sp] = [e for e in self.writers.get(sp, ())
                                if not (lo <= e[0] and e[1] <= hi)]
            self.writers[sp].append((lo, hi, me))
            self.readers[sp] = [e for e in self.readers.get(sp, ())
                                if not (lo <= e[0] and e[1] <= hi)]
        for (sp, lo, hi) in reads:
            self.readers.setdefault(sp, []).append((lo, hi, me))

    def finalize(self):
        for e in ENGINES:
            for op in self.ops[e]:
                for (se, si) in op["deps"]:
                    self.ops[se][si]["needs_inc"] = True
        for e in ENGINES:
            for op in self.ops[e]:
                if op["dma"]:
                    op["needs_inc"] = True
        self.cum = {}
        for e in ENGINES:
            c, cl = 0, []
            for op in self.ops[e]:
                if op["needs_inc"] and not op["dma"]:
                    c += 1
                cl.append(c)
            self.cum[e] = cl
        self.streams = sorted(self.stream_count)
        for e in ENGINES:
            vc = {}
            for op in self.ops[e]:
                req = {}
                for (se, si) in op["deps"]:
                    p = self.ops[se][si]
                    if p["dma"]:
                        st = p["stream"]
                        v = 16 * op["snap"].get(st, 0)
                        key = ("S", st)
                    else:
                        v = self.cum[se][si]
                        key = ("E", se)
                    if v > req.get(key, 0):
                        req[key] = v
                waits = []
                for key, v in req.items():
                    if v > vc.get(key, 0):
                        waits.append((key, v))
                        vc[key] = v
                op["waits"] = waits

    def run(self, eng_name, eng, sems, ssems):
        for op in self.ops[eng_name]:
            for ((kind, name), v) in op["waits"]:
                eng.wait_ge(sems[name] if kind == "E" else ssems[name], v)
            instr = op["fn"](eng)
            if op["needs_inc"]:
                assert instr is not None, f"needs inc but fn returned None: {op['tag']}"
                if op["dma"]:
                    instr.then_inc(ssems[op["stream"]], 16)
                else:
                    instr.then_inc(sems[eng_name], 1)


def build_program(debug=False):
    nc = bass.Bass("TRN2", target_bir_lowering=False, debug=False)

    x_in = nc.dram_tensor("x", [S, D], F32, kind="ExternalInput").ap()
    w_in = nc.dram_tensor("w_in", [D, PROJ], F32, kind="ExternalInput").ap()
    b_in = nc.dram_tensor("b_in", [1, PROJ], F32, kind="ExternalInput").ap()
    convw_h = nc.dram_tensor("conv_w", [HBCF, 4], F32, kind="ExternalInput").ap()
    dtb_h = nc.dram_tensor("dt_bias", [NH, 1], F32, kind="ExternalInput").ap()
    alog_h = nc.dram_tensor("a_log", [NH, 1], F32, kind="ExternalInput").ap()
    drow_h = nc.dram_tensor("d_head", [1, NH], F32, kind="ExternalInput").ap()
    nw_h = nc.dram_tensor("nw", [1, GATE], F32, kind="ExternalInput").ap()
    w_out = nc.dram_tensor("w_out", [GATE, D], F32, kind="ExternalInput").ap()
    b_out = nc.dram_tensor("b_out", [1, D], F32, kind="ExternalInput").ap()
    w_ffn = nc.dram_tensor("w_ffn", [D, D], F32, kind="ExternalInput").ap()
    eye_bf_h = nc.dram_tensor("eye_bf", [128, 128], BF16, kind="ExternalInput").ap()
    eye32_h = nc.dram_tensor("eye32", [128, 128], F32, kind="ExternalInput").ap()
    utri_h = nc.dram_tensor("utri", [128, 128], F32, kind="ExternalInput").ap()
    trim_h = nc.dram_tensor("trimask", [128, 256], BF16, kind="ExternalInput").ap()
    out_h = nc.dram_tensor("out", [S, D], F32, kind="ExternalOutput").ap()
    wdt_scr = nc.dram_tensor("wdt_scr", [4, 16, 128, 512], BF16, kind="Internal").ap()
    scr_win = nc.dram_tensor("scr_win", [16, 128, PROJ], BF16, kind="Internal").ap()
    scr_wo = nc.dram_tensor("scr_wo", [32, 128, 2048], BF16, kind="Internal").ap()
    dbg = {}
    if debug:
        for nm, shp in [("d_xbc", [HBCF, TG]), ("d_dt", [NH, TG]), ("d_acum", [NH, TG]),
                        ("d_y", [4 * 128, GATE]), ("d_x1", [TG, D]), ("d_xn", [TG, D]),
                        ("d_x2n", [128, D]), ("d_xdeq", [128, D]), ("d_wf", [128, 512])]:
            dbg[nm] = nc.dram_tensor(nm, shp, F32, kind="ExternalOutput").ap()

    P = Prog()

    import contextlib
    with contextlib.ExitStack() as _st:
        xres = _st.enter_context(nc.sbuf_tensor("xres", [128, 4 * D], F32))
        carry = _st.enter_context(nc.sbuf_tensor("carry", [128, NH * HP], BF16))
        tails = _st.enter_context(nc.sbuf_tensor("tails", [128, 34 * 3], BF16))
        convw = _st.enter_context(nc.sbuf_tensor("convw", [128, 34 * 4], F32))
        dtb = _st.enter_context(nc.sbuf_tensor("dtb", [NH, 1], F32))
        alog = _st.enter_context(nc.sbuf_tensor("alog", [NH, 1], F32))
        expA = _st.enter_context(nc.sbuf_tensor("expA", [NH, 1], F32))
        drow_s = _st.enter_context(nc.sbuf_tensor("drow_s", [1, NH], F32))
        drep = _st.enter_context(nc.sbuf_tensor("drep", [128, NH], F32))
        dtraw = _st.enter_context(nc.sbuf_tensor("dtraw", [NH, TG], F32))
        sp1 = _st.enter_context(nc.sbuf_tensor("sp1", [NH, TG], F32))
        dtv = _st.enter_context(nc.sbuf_tensor("dtv", [NH, TG], F32))
        abt = _st.enter_context(nc.sbuf_tensor("abt", [NH, TG], F32))
        acum = _st.enter_context(nc.sbuf_tensor("acum", [NH, TG], F32))
        decf = _st.enter_context(nc.sbuf_tensor("decf", [NH, LC], F32))
        decf2 = _st.enter_context(nc.sbuf_tensor("decf2", [NH, LC], F32))
        ddf = _st.enter_context(nc.sbuf_tensor("ddf", [NH, LC], F32))
        acT = _st.enter_context(nc.sbuf_tensor("acT", [128, 128], F32))
        acumT = _st.enter_context(nc.sbuf_tensor("acumT", [128, 128], F32))
        nacT = _st.enter_context(nc.sbuf_tensor("nacT", [128, 128], F32))
        sdT = _st.enter_context(nc.sbuf_tensor("sdT", [128, 128], F32))
        dtT = _st.enter_context(nc.sbuf_tensor("dtT", [128, 128], F32))
        ddT = _st.enter_context(nc.sbuf_tensor("ddT", [128, 128], F32))
        exptot = _st.enter_context(nc.sbuf_tensor("exptot", [128, NH], F32))
        alastr = _st.enter_context(nc.sbuf_tensor("alastr", [1, NH], F32))
        statc = _st.enter_context(nc.sbuf_tensor("statc", [128, 84], F32))
        sqd = _st.enter_context(nc.sbuf_tensor("sqd", [128, 1024], BF16))

        ksqd = lambda: ("sqd", 0, 1 << 30)
        constc = _st.enter_context(nc.sbuf_tensor("constc", [128, 2], F32))
        nwrep = _st.enter_context(nc.sbuf_tensor("nwrep", [128, GATE], BF16))
        b_bf = _st.enter_context(nc.sbuf_tensor("b_bf", [1, PROJ], BF16))
        outb_bf = _st.enter_context(nc.sbuf_tensor("outb_bf", [1, D], BF16))
        ones_bf = _st.enter_context(nc.sbuf_tensor("ones_bf", [1, 512], BF16))
        ones3bf = _st.enter_context(nc.sbuf_tensor("ones3bf", [3, 128], BF16))
        ones32 = _st.enter_context(nc.sbuf_tensor("ones32", [1, 128], F32))
        ones128 = _st.enter_context(nc.sbuf_tensor("ones128", [128, 128], F32))
        eye_bf = _st.enter_context(nc.sbuf_tensor("eye_bf_s", [128, 128], BF16))
        eye32 = _st.enter_context(nc.sbuf_tensor("eye32_s", [128, 128], F32))
        utri = _st.enter_context(nc.sbuf_tensor("utri_s", [128, 128], F32))
        trim = _st.enter_context(nc.sbuf_tensor("trim", [128, 256], BF16))
        gsb = _st.enter_context(nc.sbuf_tensor("gsb", [128, 512], BF16))
        AR = _st.enter_context(nc.sbuf_tensor("arena", [128, 114 * 1024], mybir.dt.uint8))
        pb0 = _st.enter_context(nc.psum_tensor("pb0", [128, 512], F32))
        pb1 = _st.enter_context(nc.psum_tensor("pb1", [128, 512], F32))
        pb2 = _st.enter_context(nc.psum_tensor("pb2", [128, 512], F32))
        pb3 = _st.enter_context(nc.psum_tensor("pb3", [128, 512], F32))
        pb4 = _st.enter_context(nc.psum_tensor("pb4", [128, 512], F32))
        pb5 = _st.enter_context(nc.psum_tensor("pb5", [128, 512], F32))
        pb6 = _st.enter_context(nc.psum_tensor("pb6", [128, 512], F32))
        pb7 = _st.enter_context(nc.psum_tensor("pb7", [128, 512], F32))
        s_sync = _st.enter_context(nc.semaphore("s_sync"))
        s_tensor = _st.enter_context(nc.semaphore("s_tensor"))
        s_vector = _st.enter_context(nc.semaphore("s_vector"))
        s_scalar = _st.enter_context(nc.semaphore("s_scalar"))
        s_gpsimd = _st.enter_context(nc.semaphore("s_gpsimd"))
        ssem_pool = [_st.enter_context(nc.semaphore(f"s_dq{i}")) for i in range(40)]
        block = _st.enter_context(nc.Block())
        sems = {"sync": s_sync, "tensor": s_tensor, "vector": s_vector,
                "scalar": s_scalar, "gpsimd": s_gpsimd}
        pbank = [pb0, pb1, pb2, pb3, pb4, pb5, pb6, pb7]
        pb4r = [pb0, pb1, pb4, pb5]
        def kP4(j):
            return (f"P{[0, 1, 4, 5][j]}", 0, 512)

        def arv(off_kb, nbytes, dt_):
            return AR[0:128, off_kb * 1024: off_kb * 1024 + nbytes].bitcast(dt_)
        def kAR(off_kb, nbytes):
            return ("AR", off_kb * 1024, off_kb * 1024 + nbytes)

        XBC = [arv(i, 1024, BF16) for i in range(34)]
        XNT = [arv(34 + i, 1024, BF16) for i in range(16)]
        Yt = [arv(50 + 8 * i, 8192, BF16) for i in range(4)]
        Xt = [arv(82 + 8 * i, 8192, BF16) for i in range(2)]
        AROW2 = [AR[0:3, 98 * 1024: 98 * 1024 + 4 * 1024].bitcast(BF16),
                 AR[0:3, 106 * 1024: 106 * 1024 + 4 * 1024].bitcast(BF16)]
        WGATE2 = [arv(16, 16384, BF16), arv(0, 16384, BF16)]
        kWGATE2 = [kAR(16, 16384), kAR(0, 16384)]
        XGT = [arv(8 * i, 8192, BF16) for i in range(4)]
        X2N = arv(82, 8192, F32)
        Q8 = arv(90, 2048, FP8)
        XDEQ = arv(92, 4096, BF16)
        XDT = [arv(16 + i, 1024, BF16) for i in range(16)]
        WOUT2 = [arv(82, 32768, BF16), arv(50, 32768, BF16)]
        kWOUT2 = [kAR(82, 32768), kAR(50, 32768)]
        WFFN2 = [arv(34, 16384, BF16), arv(0, 16384, BF16)]
        kWFFN2 = [kAR(34, 16384), kAR(0, 16384)]
        WFASM = [arv(50 + i, 1024, BF16) for i in range(16)]
        NWROW = AR[0:1, 98 * 1024: 98 * 1024 + GATE * 4].bitcast(F32)
        WEAGER = [arv(82, 16384, BF16), arv(98, 16384, BF16)]
        kWEAGER = [kAR(82, 16384), kAR(98, 16384)]

        def arbuf(off, nbytes, dt_):
            esz = {F32: 4, BF16: 2, FP8: 1}[dt_]
            view = AR[0:128, off: off + nbytes].bitcast(dt_)
            def key(lo=0, hi=None):
                h = nbytes // esz if hi is None else hi
                return ("AR", off + lo * esz, off + h * esz)
            return view, key

        xn1, kxn1 = arbuf(4096, 4096, BF16)
        xn1c = arv(50, 4096, BF16)
        xn1b = arv(86, 4096, BF16)
        preE, kpreE = arbuf(34 * 1024, 3072, F32)
        Et, kEt = arbuf(34 * 1024 + 3072, 1536, BF16)
        Mt, kMt = arbuf(34 * 1024 + 4608, 1536, BF16)
        Gt, kGt = arbuf(34 * 1024 + 6144, 768, BF16)
        Bt, kBt = arbuf(34 * 1024 + 6912, 512, BF16)
        xd2, kxd2 = arbuf(34 * 1024 + 7424, 512, BF16)
        asp1f, kasp1 = arbuf(42 * 1024, 512, BF16)
        asp2f, kasp2 = arbuf(42 * 1024 + 512, 512, BF16)
        asp3f, kasp3 = arbuf(43 * 1024, 512, BF16)
        asprf, kaspr = arbuf(43 * 1024 + 1024, 1024, F32)
        asp1 = asp1f[0:NH, :]
        asp2 = asp2f[0:NH, :]
        asp3 = asp3f[0:NH, :]
        aspr = asprf[0:NH, :]
        pc2l = [arbuf(56 * 1024, 1040, BF16), arbuf(58 * 1024, 1040, BF16)]
        cacc2l = [arbuf(60 * 1024, 1024, BF16), arbuf(62 * 1024, 1024, BF16)]

        kXBC = [kAR(i, 1024) for i in range(34)]
        kXNT = [kAR(34 + i, 1024) for i in range(16)]
        kY = [kAR(50 + 8 * i, 8192) for i in range(4)]
        kXt = [kAR(82 + 8 * i, 8192) for i in range(2)]
        kAROW2 = [kAR(98, 8 * 1024), kAR(106, 8 * 1024)]
        kXGT = [kAR(8 * i, 8192) for i in range(4)]
        kX2N = kAR(82, 8192)
        kQ8 = kAR(90, 2048)
        kXDEQ = kAR(92, 4096)
        kXDT = [kAR(16 + i, 1024) for i in range(16)]
        kWFASM = [kAR(50 + i, 1024) for i in range(16)]

        ARBF = AR[0:128, :].bitcast(BF16)

        def ar3(off_bytes, stride_bytes, n_mid, n_inner):
            return bass.AP(tensor=ARBF.tensor, offset=off_bytes // 2,
                           ap=[[ARBF.ap[0][0], 128], [stride_bytes // 2, n_mid],
                               [1, n_inner]])

        def tr_batch8(srcs, dst3d, wkeys, rkeys, bank_sel, tag):
            bank = pb2 if bank_sel == 0 else pb3
            kb = kP(2) if bank_sel == 0 else kP(3)
            for j, (src_ap, rk) in enumerate(zip(srcs, rkeys)):
                pslot = bank.bitcast(BF16)[:, j * 128:(j + 1) * 128]
                transp(pslot, src_ap, eye_bf[:], [rk], kb, tag=f"{tag}_t{j}")
            act(dst3d, bank.bitcast(BF16)[:, 0:1024].rearrange("p (a b) -> p a b", a=8),
                AF.Copy, [kb], wkeys, tag=f"{tag}_e")

        def kP(i, lo=0, hi=512):
            return (f"P{i}", 0, 512)

        def kT(name, lo=0, hi=1 << 30):
            return (name, lo, hi)

        def dma(eng, out, in_, reads, writes, tag="", force_inc=False, stream=None):
            P.emit(eng, lambda e, out=out, in_=in_: e.dma_start(out=out, in_=in_),
                   reads=reads, writes=writes, dma=True, force_inc=force_inc, tag=tag,
                   stream=stream)

        def mm(psum_ap, pairs, reads, wkey, tag=""):
            def fn(t, psum_ap=psum_ap, pairs=pairs):
                n = len(pairs)
                ins = None
                for i, (l, r) in enumerate(pairs):
                    ins = t.matmul(psum_ap, l, r, start=(i == 0), stop=(i == n - 1))
                return ins
            P.emit("tensor", fn, reads=reads, writes=[wkey], tag=tag)

        def transp(psum_ap, in_ap, ident, reads, wkey, tag=""):
            P.emit("tensor", lambda t, a=psum_ap, b=in_ap, c=ident: t.transpose(a, b, c),
                   reads=list(reads) + [kT("eye")], writes=[wkey], tag=tag)

        def act(out, in_, func, reads, writes, bias=0.0, scale=1.0, accum_out=None, tag=""):
            if accum_out is None:
                P.emit("scalar",
                       lambda s, out=out, in_=in_, func=func, bias=bias, scale=scale:
                       s.activation(out=out, in_=in_, func=func, bias=bias, scale=scale),
                       reads=reads, writes=writes, tag=tag)
            else:
                def fn(s, out=out, in_=in_, func=func, bias=bias, scale=scale, ao=accum_out):
                    s.activation(out=out, in_=in_, func=func, bias=bias, scale=scale,
                                 accum_out=ao)
                    return s.drain()
                P.emit("scalar", fn, reads=reads, writes=writes, tag=tag)

        def ts(eng, out, in0, s1, s2, op0, op1, reads, writes, tag=""):
            if s2 is None and op1 is None:
                P.emit(eng, lambda e, out=out, in0=in0, s1=s1, op0=op0:
                       e.tensor_scalar(out=out, in0=in0, scalar1=s1, scalar2=None, op0=op0),
                       reads=reads, writes=writes, tag=tag)
            else:
                P.emit(eng, lambda e, out=out, in0=in0, s1=s1, s2=s2, op0=op0, op1=op1:
                       e.tensor_scalar(out=out, in0=in0, scalar1=s1, scalar2=s2, op0=op0, op1=op1),
                       reads=reads, writes=writes, tag=tag)

        def stt(out, in0, sc, in1, op0, op1, reads, writes, tag=""):
            P.emit("vector",
                   lambda v, out=out, in0=in0, sc=sc, in1=in1, op0=op0, op1=op1:
                   v.scalar_tensor_tensor(out=out, in0=in0, scalar=sc, in1=in1, op0=op0, op1=op1),
                   reads=reads, writes=writes, tag=tag)

        def tt(eng, out, in0, in1, op, reads, writes, tag=""):
            P.emit(eng,
                   lambda e, out=out, in0=in0, in1=in1, op=op:
                   e.tensor_tensor(out=out, in0=in0, in1=in1, op=op),
                   reads=reads, writes=writes, tag=tag)

        def cp(eng, out, in_, reads, writes, tag=""):
            P.emit(eng, lambda e, out=out, in_=in_: e.tensor_copy(out=out, in_=in_),
                   reads=reads, writes=writes, tag=tag)

        def vred(out, in_, op_abs, wkeys, rkeys, tag=""):
            P.emit("vector",
                   lambda v, out=out, in_=in_: (
                       v.reduce_max(out=out, in_=in_, axis=AX.X, apply_absolute_value=True)
                       if op_abs else v.reduce_sum(out=out, in_=in_, axis=AX.X)),
                   reads=rkeys, writes=wkeys, tag=tag)

        dma("sync", eye_bf[:], eye_bf_h, [], [kT("eye")], tag="eye")
        dma("sync", eye32[:], eye32_h, [], [kT("eye")], tag="eye32")
        dma("sync", utri[:], utri_h, [], [kT("utri")], tag="utri")
        dma("sync", trim[:], trim_h, [], [kT("trim")], tag="trim")
        dma("sync", convw[0:128, :].rearrange("p (i k) -> p i k", k=4),
            convw_h.rearrange("(i p) k -> p i k", p=128), [], [kT("convw")], tag="convw")
        dma("sync", dtb[:], dtb_h, [], [kT("dtb")], tag="dtb")
        dma("sync", alog[:], alog_h, [], [kT("alog")], tag="alog")
        dma("sync", drow_s[:], drow_h, [], [kT("drow")], tag="drow")
        P.emit("gpsimd", lambda g: g.memset(ones_bf[:], 1.0), writes=[kT("onesb")], tag="ones_bf")
        P.emit("gpsimd", lambda g: g.memset(ones3bf[:], 1.0), writes=[kT("ones3")], tag="ones3")
        P.emit("gpsimd", lambda g: g.memset(ones32[:], 1.0), writes=[kT("ones32")], tag="ones32")
        P.emit("gpsimd", lambda g: g.memset(ones128[:], 1.0), writes=[kT("ones128")], tag="ones128")
        P.emit("gpsimd", lambda g: g.memset(constc[:, 0:1], 1e-5), writes=[kT("constc")], tag="eps")
        P.emit("gpsimd", lambda g: g.memset(constc[:, 1:2], 1.0), writes=[kT("constc")], tag="one")
        P.emit("gpsimd", lambda g: g.memset(carry[:], 0.0), writes=[kT("carry", 0, NH * HP)], tag="carry0")
        P.emit("gpsimd", lambda g: g.memset(tails[:], 0.0), writes=[kT("tails", 0, 102)], tag="tails0")
        dma("gpsimd", b_bf[:], b_in, [], [kT("b_bf")], tag="b_bf")
        dma("gpsimd", outb_bf[:], b_out, [], [kT("outb")], tag="outb")
        act(expA[:], alog[:], AF.Exp, [kT("alog")], [kT("expA")], tag="expA")
        mm(pb6[:, 0:NH], [(ones32[:], drow_s[0:1, :])],
           [kT("ones32"), kT("drow")], kP(6, 0, NH), tag="drep_mm")
        act(drep[:], pb6[:, 0:NH], AF.Copy, [kP(6, 0, NH)], [kT("drep")], tag="drep_ev")
        dma("sync", NWROW, nw_h, [], [kAR(98, 16 * 1024)], tag="nwrow")
        for j in range(8):
            mm(pb6[:, 0:512], [(ones32[:], NWROW[0:1, j * 512:(j + 1) * 512])],
               [kT("ones32"), kAR(98, 16 * 1024)], kP(6), tag=f"nwrep{j}")
            act(nwrep[:, j * 512:(j + 1) * 512], pb6[:, 0:512], AF.Copy,
                [kP(6)], [kT("nwrep", j * 512, (j + 1) * 512)], tag=f"nwrep_ev{j}")

        xres_t = [xres[:, i * D:(i + 1) * D] for i in range(NTT)]
        kxres = [kT("xres", i * D, (i + 1) * D) for i in range(NTT)]

        def emit_xload(g):
            for i in range(NTT):
                dma("sync", xres_t[i], x_in[(g * NTT + i) * 128:(g * NTT + i + 1) * 128, :],
                    [], [kxres[i]], tag=f"g{g}_xld{i}", stream=f"xload{i}")

        def ln_stats(sxo, ssqo, nmo, rso, src, ksrc, tag):
            vred(statc[:, sxo:sxo + 1], src, False,
                 [kT("statc", sxo, sxo + 1)], [ksrc], tag=tag + "_sx")
            act(sqd[:], src[:, 0:1024], AF.Square, [ksrc],
                [ksqd(), kT("statc", ssqo, ssqo + 1)],
                accum_out=statc[:, ssqo:ssqo + 1], tag=tag + "_ssqa")
            act(sqd[:], src[:, 1024:2048], AF.Square, [ksrc],
                [ksqd(), kT("statc", 80, 81)],
                accum_out=statc[:, 80:81], tag=tag + "_ssqb")
            tt("vector", statc[:, ssqo:ssqo + 1], statc[:, ssqo:ssqo + 1],
               statc[:, 80:81], ALU.add,
               [kT("statc", ssqo, ssqo + 1), kT("statc", 80, 81)],
               [kT("statc", ssqo, ssqo + 1)], tag=tag + "_ssqc")
            ts("vector", statc[:, nmo:nmo + 1], statc[:, sxo:sxo + 1], -1.0 / D, None,
               ALU.mult, None, [kT("statc", sxo, sxo + 1)], [kT("statc", nmo, nmo + 1)],
               tag=tag + "_nm")
            ts("vector", statc[:, rso:rso + 1], statc[:, ssqo:ssqo + 1], 1.0 / D, None,
               ALU.mult, None, [kT("statc", ssqo, ssqo + 1)], [kT("statc", rso, rso + 1)],
               tag=tag + "_m2")
            stt(statc[:, rso:rso + 1], statc[:, nmo:nmo + 1], statc[:, nmo:nmo + 1],
                statc[:, rso:rso + 1], ALU.mult, ALU.subtract,
                [kT("statc", nmo, nmo + 1), kT("statc", rso, rso + 1)],
                [kT("statc", rso, rso + 1)], tag=tag + "_var")
            act(statc[:, rso:rso + 1], statc[:, rso:rso + 1], AF.Sqrt,
                [kT("statc", rso, rso + 1), kT("constc")], [kT("statc", rso, rso + 1)],
                bias=constc[:, 0:1], scale=-1.0, tag=tag + "_sqrt")
            P.emit("vector", lambda v, o=statc[:, rso:rso + 1]: v.reciprocal(out=o, in_=o),
                   reads=[kT("statc", rso, rso + 1)], writes=[kT("statc", rso, rso + 1)],
                   tag=tag + "_rc")

        def ln1_tile(g, i):
            ln_stats(i, 4 + i, 8 + i, 12 + i, xres_t[i], kxres[i], f"g{g}_ln1_{i}")

        def emit_ln1(g, stats=True):
            if stats:
                for i in range(NTT):
                    ln1_tile(g, i)
            for i in range(NTT):
                xi, kxi = (xn1, kxn1()) if i % 2 == 0 else (xn1c, kAR(50, 4096))
                ts("vector", xi[:], xres_t[i], statc[:, 8 + i:9 + i], statc[:, 12 + i:13 + i],
                   ALU.add, ALU.mult,
                   [kxres[i], kT("statc", 8 + i, 9 + i), kT("statc", 12 + i, 13 + i)],
                   [kxi], tag=f"g{g}_ln1ap{i}")
                for kb in range(2):
                    k0 = kb * 8
                    tr_batch8(
                        [xi[:, (k0 + j) * 128:(k0 + j + 1) * 128] for j in range(8)],
                        ar3(kXNT[k0][1] + i * 256, 1024, 8, 128),
                        [("AR", kXNT[k0 + j][1] + i * 256,
                          kXNT[k0 + j][1] + (i + 1) * 256) for j in range(8)],
                        [kxi] * 8, (i * 2 + kb) % 2, tag=f"g{g}_xnT{i}_{kb}")
                if debug and g == 0:
                    dma("gpsimd", dbg["d_xn"][i * 128:(i + 1) * 128, :], xi[:],
                        [kxi], [kT("dbgxn")], tag=f"dxn{i}", force_inc=True, stream="dbg")


        emit_xload(0)
        emit_ln1(0)


        for g in range(NG):
            if g > 0:
                emit_ln1(g, stats=False)

            for pi in range(35):
                wb = pi // 4
                wbuf = WEAGER[wb % 2]
                kwbuf = kWEAGER[wb % 2]
                if pi % 4 == 0:
                    bw = min(512, PROJ - GATE - wb * 512)
                    for h in range(2):
                        kwh = ("AR", kwbuf[1] + h * 8192, kwbuf[1] + (h + 1) * 8192)
                        if g == 0:
                            dma("gpsimd",
                                wbuf[:, h * 4096:(h + 1) * 4096]
                                .rearrange("p (k o) -> p k o", o=512)[:, :, 0:bw],
                                w_in[h * 1024:(h + 1) * 1024,
                                     GATE + wb * 512: GATE + wb * 512 + bw]
                                .rearrange("(k p) o -> p k o", p=128),
                                [], [kwh], tag=f"g{g}_wldd{wb}_{h}",
                                stream=f"we{wb % 2}{'b' if h else ''}")
                        else:
                            dma("sync",
                                wbuf[:, h * 4096:(h + 1) * 4096]
                                .rearrange("p (k o) -> p k o", o=512)[:, :, 0:bw],
                                scr_win[h * 8:(h + 1) * 8, :,
                                        GATE + wb * 512: GATE + wb * 512 + bw]
                                .rearrange("k p o -> p k o"),
                                [kT("scre", h * 8, (h + 1) * 8)], [kwh],
                                tag=f"g{g}_wld{wb}_{h}",
                                stream=f"we{wb % 2}{'b' if h else ''}")
                width = 128 if pi < 34 else 64
                col0 = GATE + pi * 128
                o0 = (pi % 4) * 128
                pb = pb4r[pi % 4]
                pairs = [(wbuf[:, k * 512 + o0: k * 512 + o0 + width], XNT[k][:, :])
                         for k in range(16)]
                pairs.append((b_bf[0:1, col0:col0 + width], ones_bf[0:1, 0:512]))
                mm(pb[0:width, :], pairs,
                   [kT("b_bf"), kT("onesb"), kwbuf] + [kXNT[k] for k in range(16)],
                   kP4(pi % 4), tag=f"g{g}_proj{pi}")
                if pi < 34:
                    pc, kpc = pc2l[pi % 2]
                    cacc, kcacc = cacc2l[pi % 2]
                    cp("vector", pc[:, 0:3], tails[:, pi * 3:(pi + 1) * 3],
                       [kT("tails", pi * 3, (pi + 1) * 3)], [kpc(0, 3)], tag=f"g{g}_tin{pi}")
                    act(pc[:, 3:515], pb[:, 0:512], AF.Copy, [kP4(pi % 4)], [kpc(3, 515)],
                        tag=f"g{g}_pcev{pi}")
                    cp("vector", tails[:, pi * 3:(pi + 1) * 3], pc[:, 512:515],
                       [kpc(3, 515)], [kT("tails", pi * 3, (pi + 1) * 3)], tag=f"g{g}_tou{pi}")
                    ts("vector", cacc[:], pc[:, 0:512], convw[0:128, pi * 4:pi * 4 + 1], None,
                       ALU.mult, None, [kpc(0, 515), kT("convw")], [kcacc()],
                       tag=f"g{g}_cv0_{pi}")
                    for kk in range(1, 4):
                        stt(cacc[:], pc[:, kk:512 + kk], convw[0:128, pi * 4 + kk:pi * 4 + kk + 1],
                            cacc[:], ALU.mult, ALU.add,
                            [kpc(0, 515), kT("convw"), kcacc()], [kcacc()],
                            tag=f"g{g}_cv{kk}_{pi}")
                    act(XBC[pi][:, :], cacc[:], AF.Silu, [kcacc()], [kXBC[pi]],
                        tag=f"g{g}_silu{pi}")
                else:
                    act(dtraw[:, :], pb[0:NH, :], AF.Copy, [kP4(pi % 4)], [kT("dtraw")],
                        tag=f"g{g}_dtev")

            if debug and g == 0:
                for pi in range(34):
                    dma("gpsimd", dbg["d_xbc"][pi * 128:(pi + 1) * 128, :], XBC[pi][:, :],
                        [kXBC[pi]], [kT("dbg1")], tag=f"dxbc{pi}", force_inc=True, stream="dbg")

            if g == 0:
                for k in range(16):
                    dma("gpsimd", scr_win[k], w_in[k * 128:(k + 1) * 128, :],
                        [], [kT("scre", k, k + 1)], tag=f"wp_{k}", stream=f"wprep{k % 4}")
                for k2 in range(32):
                    dma("gpsimd", scr_wo[k2], w_out[k2 * 128:(k2 + 1) * 128, :],
                        [], [kT("scro", k2, k2 + 1)], tag=f"wpo_{k2}", stream=f"wprep{k2 % 4}")
                for oh in range(4):
                    for j in range(4):
                        i = oh * 4 + j
                        dma("sync", X2N[:, :], w_ffn[i * 128:(i + 1) * 128, :],
                            [], [kX2N], tag=f"wf_ld{i}")
                        for b in range(16):
                            vred(statc[:, 48 + b:49 + b], X2N[:, b * 128:(b + 1) * 128], True,
                                 [kT("statc", 48, 64)], [kX2N], tag=f"wf_mx{i}_{b}")
                        ts("vector", statc[:, 48:64], statc[:, 48:64], 2.0 / 448.0, None, ALU.mult, None,
                           [kT("statc", 48, 64)], [kT("statc", 48, 64)], tag=f"wf_s{i}")
                        P.emit("vector", lambda v: v.reciprocal(out=statc[:, 64:80], in_=statc[:, 48:64]),
                               reads=[kT("statc", 48, 64)], writes=[kT("statc", 64, 80)], tag=f"wf_si{i}")
                        for b in range(16):
                            kQ8b = ("AR", 90 * 1024 + b * 128, 90 * 1024 + (b + 1) * 128)
                            kXDQb = ("AR", 92 * 1024 + b * 256, 92 * 1024 + (b + 1) * 256)
                            ts("vector", Q8[:, b * 128:(b + 1) * 128], X2N[:, b * 128:(b + 1) * 128],
                               statc[:, 64 + b:65 + b], None, ALU.mult, None,
                               [kX2N, kT("statc", 64, 80)], [kQ8b], tag=f"wf_q{i}_{b}")
                            act(XDEQ[:, b * 128:(b + 1) * 128], Q8[:, b * 128:(b + 1) * 128],
                                AF.Copy, [kQ8b, kT("statc", 48, 64)], [kXDQb],
                                scale=statc[:, 48 + b:49 + b], tag=f"wf_dq{i}_{b}")
                        for k in range(16):
                            pslot = pb2.bitcast(BF16)[:, (k % 8) * 128:(k % 8 + 1) * 128]
                            transp(pslot, XDEQ[:, k * 128:(k + 1) * 128], eye_bf[:],
                                   [("AR", 92 * 1024 + k * 256, 92 * 1024 + (k + 1) * 256)],
                                   kP(2, (k % 8) * 64, (k % 8) * 64 + 64), tag=f"wf_tr{i}_{k}")
                            act(WFASM[k][:, j * 128:(j + 1) * 128], pslot, AF.Copy,
                                [kP(2, (k % 8) * 64, (k % 8) * 64 + 64)], [kWFASM[k]], tag=f"wf_ev{i}_{k}")
                    for k in range(16):
                        dma("gpsimd", wdt_scr[oh, k], WFASM[k][:, :],
                            [kWFASM[k]], [kT("wdtscr", oh * 16 + k, oh * 16 + k + 1)], tag=f"wf_st{oh}_{k}", stream=f"scr{k % 2}")
            ts("vector", sp1[:], dtraw[:], dtb[:, 0:1], None, ALU.add, None,
               [kT("dtraw"), kT("dtb")], [kT("sp1")], tag=f"g{g}_dtz")
            ts("vector", dtraw[:], sp1[:], 0.0, None, ALU.max, None,
               [kT("sp1")], [kT("dtraw")], tag=f"g{g}_relu")
            act(dtv[:], sp1[:], AF.Abs, [kT("sp1")], [kT("dtv")], tag=f"g{g}_abs")
            act(dtv[:], dtv[:], AF.Exp, [kT("dtv")], [kT("dtv")], scale=-1.0, tag=f"g{g}_exp")
            act(dtv[:], dtv[:], AF.Ln, [kT("dtv"), kT("constc")], [kT("dtv")], bias=constc[0:NH, 1:2], tag=f"g{g}_ln")
            tt("vector", dtv[:], dtv[:], dtraw[:], ALU.add, [kT("dtv"), kT("dtraw")], [kT("dtv")],
               tag=f"g{g}_sp")
            ts("vector", abt[:], dtv[:], expA[:, 0:1], -1.0, ALU.mult, ALU.mult,
               [kT("dtv"), kT("expA")], [kT("abt")], tag=f"g{g}_abt")
            if debug and g == 0:
                dma("sync", dbg["d_dt"], dtv[:], [kT("dtv")], [kT("dbg2")], tag="ddt",
                    force_inc=True, stream="dbg")

            for cc in range(NCH):
                ccol = cc * LC
                first_chunk = (g == 0 and cc == 0)

                for lb in range(2):
                    transp(pb2[:, lb * 64:(lb + 1) * 64],
                           abt[:, ccol + lb * 128: ccol + (lb + 1) * 128],
                           eye32[0:64, 0:64], [kT("abt")], kP(2, lb * 64, (lb + 1) * 64), tag=f"acT{lb}")
                    act(acT[:, lb * 64:(lb + 1) * 64], pb2[:, lb * 64:(lb + 1) * 64], AF.Copy,
                        [kP(2, lb * 64, (lb + 1) * 64)], [kT("acT", lb * 64, (lb + 1) * 64)],
                        tag=f"acTe{lb}")
                mm(pb3[:, 384:448], [(utri[:], acT[:, 0:64])],
                   [kT("utri"), kT("acT", 0, 64)], kP(3, 384, 448), tag="cums0")
                mm(pb3[:, 448:512], [(ones128[:], acT[:, 0:64]), (utri[:], acT[:, 64:128])],
                   [kT("ones128"), kT("acT", 0, 128)], kP(3, 448, 512), tag="cums1")
                for lb in range(2):
                    pslot = pb3[:, 384 + lb * 64: 448 + lb * 64]
                    kps = kP(3, 384 + lb * 64, 448 + lb * 64)
                    act(acumT[:, lb * 64:(lb + 1) * 64], pslot, AF.Copy, [kps],
                        [kT("acumT", lb * 64, (lb + 1) * 64)], tag=f"acumTe{lb}")
                    act(sdT[:, lb * 64:(lb + 1) * 64], pslot, AF.Exp, [kps],
                        [kT("sdT", lb * 64, (lb + 1) * 64)], tag=f"sdT{lb}")
                    ts("vector", nacT[:, lb * 64:(lb + 1) * 64], acumT[:, lb * 64:(lb + 1) * 64],
                       -1.0, None, ALU.mult, None, [kT("acumT", lb * 64, (lb + 1) * 64)],
                       [kT("nacT", lb * 64, (lb + 1) * 64)], tag=f"nacT{lb}")
                    transp(pb2[0:64, lb * 128:(lb + 1) * 128], acumT[:, lb * 64:(lb + 1) * 64],
                           eye32[:], [kT("acumT", lb * 64, (lb + 1) * 64)],
                           kP(2, lb * 128, (lb + 1) * 128), tag=f"afm{lb}")
                    act(acum[:, ccol + lb * 128: ccol + (lb + 1) * 128],
                        pb2[0:64, lb * 128:(lb + 1) * 128], AF.Copy,
                        [kP(2, lb * 128, (lb + 1) * 128)], [kT("acum", ccol, ccol + 256)],
                        tag=f"afme{lb}")
                cp("vector", asp1[:], acum[:, ccol:ccol + 256],
                   [kT("acum", ccol, ccol + 256)], [kasp1()], tag="as1")
                stt(aspr[:], asp1[:], -1.0, acum[:, ccol:ccol + 256], ALU.mult, ALU.add,
                    [kasp1(), kT("acum", ccol, ccol + 256)], [kaspr()], tag="as2")
                cp("vector", asp2[:], aspr[:], [kaspr()], [kasp2()], tag="as3")
                stt(aspr[:], asp2[:], -1.0, aspr[:], ALU.mult, ALU.add,
                    [kasp2(), kaspr()], [kaspr()], tag="as4")
                cp("vector", asp3[:], aspr[:], [kaspr()], [kasp3()], tag="as5")
                ts("vector", decf[:], acum[:, ccol:ccol + 256], -1.0,
                   acum[:, ccol + 255:ccol + 256], ALU.mult, ALU.add,
                   [kT("acum", ccol, ccol + 256)], [kT("decf")], tag="decf")
                act(decf2[:], decf[:], AF.Exp, [kT("decf")], [kT("decf2")], tag="decf2")
                tt("vector", ddf[:], decf2[:], dtv[:, ccol:ccol + 256], ALU.mult,
                   [kT("decf2"), kT("dtv")], [kT("ddf")], tag="ddf")
                for lb in range(2):
                    transp(pb2[:, lb * 64:(lb + 1) * 64], ddf[:, lb * 128:(lb + 1) * 128],
                           eye32[0:64, 0:64], [kT("ddf")], kP(2, lb * 64, (lb + 1) * 64), tag=f"ddT{lb}")
                    act(ddT[:, lb * 64:(lb + 1) * 64], pb2[:, lb * 64:(lb + 1) * 64], AF.Copy,
                        [kP(2, lb * 64, (lb + 1) * 64)], [kT("ddT", lb * 64, (lb + 1) * 64)],
                        tag=f"ddTe{lb}")
                    transp(pb2[:, 256 + lb * 64: 256 + (lb + 1) * 64],
                           dtv[:, ccol + lb * 128: ccol + (lb + 1) * 128], eye32[0:64, 0:64],
                           [kT("dtv")], kP(2, 256 + lb * 64, 256 + (lb + 1) * 64), tag=f"dtT{lb}")
                    act(dtT[:, lb * 64:(lb + 1) * 64], pb2[:, 256 + lb * 64:256 + (lb + 1) * 64],
                        AF.Ln, [kP(2, 256 + lb * 64, 256 + (lb + 1) * 64)],
                        [kT("dtT", lb * 64, (lb + 1) * 64)], tag=f"dtTe{lb}")
                if debug and g == 0:
                    dma("sync", dbg["d_acum"][:, ccol:ccol + 256], acum[:, ccol:ccol + 256],
                        [kT("acum", ccol, ccol + 256)], [kT("dbg3")], tag=f"dacum{cc}",
                        force_inc=True, stream="dbg")

                for lb in range(2):
                    lcol = ccol + lb * 128
                    for fbb in range(4):
                        f0 = fbb * 8
                        tr_batch8(
                            [XBC[f0 + j][:, lcol:lcol + 128] for j in range(8)],
                            ar3(kXt[lb][1] + f0 * 256, 256, 8, 128),
                            [kXt[lb]], [kXBC[f0 + j] for j in range(8)],
                            fbb % 2, tag=f"xt{lb}_{fbb}")
                    pslot = pb2.bitcast(BF16)[:, 0:128]
                    transp(pslot, XBC[32][:, lcol:lcol + 128], eye_bf[:],
                           [kXBC[32]], kP(2, 0, 64), tag=f"bt{lb}")
                    act(Bt[:, lb * 128:(lb + 1) * 128], pslot, AF.Copy, [kP(2, 0, 64)],
                        [kBt(lb * 128, (lb + 1) * 128)], tag=f"bte{lb}")

                mm(pb3[:, 0:256], [(XBC[32][:, ccol:ccol + 128], XBC[33][:, ccol:ccol + 256])],
                   [kXBC[32], kXBC[33]], kP(3, 0, 256), tag="gt0")
                mm(pb3[:, 256:384],
                   [(XBC[32][:, ccol + 128:ccol + 256], XBC[33][:, ccol + 128:ccol + 256])],
                   [kXBC[32], kXBC[33]], kP(3, 256, 384), tag="gt1")
                tt("vector", Gt[:, 0:256], pb3[:, 0:256], trim[:, 0:256], ALU.mult,
                   [kP(3, 0, 256), kT("trim")], [kGt(0, 256)], tag="gtm0")
                tt("vector", Gt[:, 256:384], pb3[:, 256:384], trim[:, 0:128], ALU.mult,
                   [kP(3, 256, 384), kT("trim")], [kGt(256, 384)], tag="gtm1")

                def stA(h):
                    hg, buf = h // 8, (h // 8) % 2
                    if h % 8 == 0:
                        for ci, aspx in enumerate((asp1, asp2, asp3)):
                            dma("gpsimd", AROW2[buf][ci:ci + 1, :],
                                aspx[hg * 8:(hg + 1) * 8, :],
                                [[kasp1(), kasp2(), kasp3()][ci]], [kAROW2[buf]],
                                tag=f"flat{hg}_{ci}", stream=f"flat{buf}")
                        dma("gpsimd", alastr[0:1, hg * 8:(hg + 1) * 8],
                            acum[hg * 8:(hg + 1) * 8, ccol + 255:ccol + 256],
                            [kT("acum", ccol, ccol + 256)], [kT("alastr", hg * 8, hg * 8 + 8)],
                            tag=f"alast{hg}", stream=f"flat{buf}")
                        mm(pb2[:, 384 + hg * 8:392 + hg * 8],
                           [(ones32[:], alastr[0:1, hg * 8:(hg + 1) * 8])],
                           [kT("ones32"), kT("alastr", hg * 8, hg * 8 + 8)],
                           kP(2), tag=f"etot{hg}")
                        act(exptot[:, hg * 8:(hg + 1) * 8], pb2[:, 384 + hg * 8:392 + hg * 8],
                            AF.Exp, [kP(2)],
                            [kT("exptot", hg * 8, hg * 8 + 8)], tag=f"etote{hg}")
                    sl = h % 2
                    pbR = pb4 if sl == 0 else pb0
                    kR = kP(4) if sl == 0 else kP(0)
                    mm(pbR[:, 0:256],
                       [(ones3bf[:], AROW2[buf][0:3, (h % 8) * 256:(h % 8 + 1) * 256])],
                       [kT("ones3"), kAROW2[buf]], kR, tag=f"R{h}")

                def stB1(h):
                    sl = h % 2
                    pbR = pb4 if sl == 0 else pb0
                    kR = kP(4) if sl == 0 else kP(0)
                    pE = preE[:, sl * 384: sl * 384 + 384]
                    kpE = kpreE(sl * 384, sl * 384 + 384)
                    ts("vector", pE[:, 0:256], pbR[:, 0:256],
                       nacT[:, h:h + 1], 0.0, ALU.add, ALU.min,
                       [kR, kT("nacT", 0, 64)], [kpE], tag=f"pE0_{h}")
                    ts("vector", pE[:, 256:384], pbR[:, 128:256],
                       nacT[:, 64 + h:65 + h], 0.0, ALU.add, ALU.min,
                       [kR, kT("nacT", 64, 128)], [kpE], tag=f"pE1_{h}")

                def stB2(h):
                    sl = h % 2
                    pE = preE[:, sl * 384: sl * 384 + 384]
                    kpE = kpreE(sl * 384, sl * 384 + 384)
                    eE = Et[:, sl * 384: sl * 384 + 384]
                    keE = kEt(sl * 384, sl * 384 + 384)
                    act(eE[:, 0:256], pE[:, 0:256], AF.Exp, [kpE, kT("dtT", 0, 64)], [keE],
                        bias=dtT[:, h:h + 1], tag=f"E0_{h}")
                    act(eE[:, 256:384], pE[:, 256:384], AF.Exp, [kpE, kT("dtT", 64, 128)], [keE],
                        bias=dtT[:, 64 + h:65 + h], tag=f"E1_{h}")

                def stB3(h):
                    sl = h % 2
                    eE = Et[:, sl * 384: sl * 384 + 384]
                    keE = kEt(sl * 384, sl * 384 + 384)
                    mt = Mt[:, sl * 384: sl * 384 + 384]
                    kmt = kMt(sl * 384, sl * 384 + 384)
                    tt("vector", mt[:, 0:384], Gt[:, 0:384], eE[:, 0:384], ALU.mult,
                       [kGt(0, 384), keE], [kmt], tag=f"Mt_{h}")
                    for lb in range(2):
                        act(xd2[:, sl * 128 + lb * 64: sl * 128 + (lb + 1) * 64],
                            Xt[lb][:, h * 64:(h + 1) * 64], AF.Copy,
                            [kXt[lb], kT("ddT", lb * 64, (lb + 1) * 64)],
                            [kxd2(sl * 128 + lb * 64, sl * 128 + (lb + 1) * 64)],
                            scale=ddT[:, lb * 64 + h:lb * 64 + h + 1],
                            tag=f"xd2_{h}_{lb}")

                def stC(h):
                    sl = h % 2
                    pbY = pb5 if sl == 0 else pb1
                    kYd = kP(5) if sl == 0 else kP(1)
                    pbO = pb6 if sl == 0 else pb3
                    kOf = kP(6) if sl == 0 else kP(3)
                    pbS = pb7 if sl == 0 else pb2
                    kSt = kP(7) if sl == 0 else kP(2)
                    mt = Mt[:, sl * 384: sl * 384 + 384]
                    kmt = kMt(sl * 384, sl * 384 + 384)
                    mm(pbY[:, 0:64],
                       [(mt[:, 0:128], Xt[0][:, h * 64:(h + 1) * 64])],
                       [kmt, kXt[0]], kYd, tag=f"yd0_{h}")
                    mm(pbY[:, 64:128],
                       [(mt[:, 128:256], Xt[0][:, h * 64:(h + 1) * 64]),
                        (mt[:, 256:384], Xt[1][:, h * 64:(h + 1) * 64])],
                       [kmt, kXt[0], kXt[1]], kYd, tag=f"yd1_{h}")
                    if not first_chunk:
                        mm(pbO[:, 0:64],
                           [(XBC[33][:, ccol:ccol + 128], carry[:, h * 64:(h + 1) * 64])],
                           [kXBC[33], kT("carry", h * 64, (h + 1) * 64)],
                           kOf, tag=f"yo0_{h}")
                        mm(pbO[:, 64:128],
                           [(XBC[33][:, ccol + 128:ccol + 256], carry[:, h * 64:(h + 1) * 64])],
                           [kXBC[33], kT("carry", h * 64, (h + 1) * 64)],
                           kOf, tag=f"yo1_{h}")
                    mm(pbS[:, 0:64],
                       [(Bt[:, 0:128], xd2[:, sl * 128: sl * 128 + 64]),
                        (Bt[:, 128:256], xd2[:, sl * 128 + 64: sl * 128 + 128])],
                       [kBt(0, 256), kxd2(sl * 128, sl * 128 + 128)],
                       kSt, tag=f"st_{h}")

                def stD(h):
                    sl = h % 2
                    pbY = pb5 if sl == 0 else pb1
                    kYd = kP(5) if sl == 0 else kP(1)
                    pbO = pb6 if sl == 0 else pb3
                    kOf = kP(6) if sl == 0 else kP(3)
                    pbS = pb7 if sl == 0 else pb2
                    kSt = kP(7) if sl == 0 else kP(2)
                    yv3 = AR[0:128, kY[2 * cc][1] + h * 128: kY[2 * cc][1] + h * 128 + 128]\
                        .bitcast(BF16).rearrange("p (lb o) -> p lb o", lb=1)
                    yv = bass.AP(
                        tensor=yv3.tensor, offset=yv3.offset,
                        ap=[[yv3.ap[0][0], 128], [4096, 2], [1, 64]])
                    xv3 = AR[0:128, kXt[0][1] + h * 128: kXt[0][1] + h * 128 + 128]\
                        .bitcast(BF16)
                    xv = bass.AP(
                        tensor=xv3.tensor, offset=xv3.offset,
                        ap=[[xv3.ap[0][0], 128], [4096, 2], [1, 64]])
                    kysf = ("AR", kY[2 * cc][1] + h * 128, kY[2 * cc + 1][1] + (h + 1) * 128)
                    stt(yv, xv, drep[:, h:h + 1],
                        pbY[:, 0:128].rearrange("p (a b) -> p a b", a=2),
                        ALU.mult, ALU.add,
                        [kXt[0], kXt[1], kT("drep"), kYd], [kysf], tag=f"ev1_{h}")
                    for lb in range(2):
                        ysl = Yt[2 * cc + lb][:, h * 64:(h + 1) * 64]
                        kys = ("AR", kY[2 * cc + lb][1] + h * 128,
                               kY[2 * cc + lb][1] + (h + 1) * 128)
                        if not first_chunk:
                            stt(ysl, pbO[:, lb * 64:(lb + 1) * 64],
                                sdT[:, lb * 64 + h:lb * 64 + h + 1], ysl,
                                ALU.mult, ALU.add,
                                [kOf, kT("sdT", lb * 64, (lb + 1) * 64), kys],
                                [kys], tag=f"ev2_{h}_{lb}")
                    stt(carry[:, h * 64:(h + 1) * 64], carry[:, h * 64:(h + 1) * 64],
                        exptot[:, h:h + 1], pbS[:, 0:64],
                        ALU.mult, ALU.add,
                        [kT("carry", h * 64, (h + 1) * 64), kT("exptot", h, h + 1), kSt],
                        [kT("carry", h * 64, (h + 1) * 64)], tag=f"cu_{h}")

                for it in range(NH + 5):
                    if it < NH:
                        stA(it)
                    if 1 <= it < NH + 1:
                        stB1(it - 1)
                    if 2 <= it < NH + 2:
                        stB2(it - 2)
                    if 3 <= it < NH + 3:
                        stB3(it - 3)
                    if 4 <= it < NH + 4:
                        stC(it - 4)
                    if 5 <= it < NH + 5:
                        stD(it - 5)

            if debug and g == 0:
                for i in range(4):
                    dma("gpsimd", dbg["d_y"][i * 128:(i + 1) * 128, :], Yt[i][:, :],
                        [kY[i]], [kT("dbg5")], tag=f"dy{i}", force_inc=True, stream="dbg")

            for i in range(NTT):
                xi, kxi = (xn1, kxn1()) if i % 2 == 0 else (xn1b, kAR(86, 4096))
                ts("vector", xi[:], xres_t[i], statc[:, 8 + i:9 + i], statc[:, 12 + i:13 + i],
                   ALU.add, ALU.mult,
                   [kxres[i], kT("statc", 8 + i, 9 + i), kT("statc", 12 + i, 13 + i)],
                   [kxi], tag=f"g{g}_ln1b{i}")
                for fbb in range(2):
                    f0 = fbb * 8
                    tr_batch8(
                        [xi[:, (f0 + j) * 128:(f0 + j + 1) * 128] for j in range(8)],
                        ar3(kXNT[f0][1] + i * 256, 1024, 8, 128),
                        [("AR", kXNT[f0 + j][1] + i * 256,
                          kXNT[f0 + j][1] + (i + 1) * 256) for j in range(8)],
                        [kxi] * 8, (i * 2 + fbb) % 2, tag=f"g{g}_xnU{i}_{fbb}")
            def rms_fin(i):
                act(statc[:, 16 + i:17 + i], statc[:, 16 + i:17 + i], AF.Sqrt,
                    [kT("statc", 16 + i, 17 + i), kT("constc")], [kT("statc", 16 + i, 17 + i)],
                    bias=constc[:, 0:1], scale=1.0 / GATE, tag=f"g{g}_rms_d{i}")
                P.emit("vector", lambda v, o=statc[:, 16 + i:17 + i]: v.reciprocal(out=o, in_=o),
                       reads=[kT("statc", 16 + i, 17 + i)], writes=[kT("statc", 16 + i, 17 + i)],
                       tag=f"g{g}_rms_e{i}")
                stt(Yt[i][:, :], Yt[i][:, :], statc[:, 16 + i:17 + i], nwrep[:, :],
                    ALU.mult, ALU.mult,
                    [kY[i], kT("statc", 16 + i, 17 + i), kT("nwrep", 0, GATE)], [kY[i]],
                    tag=f"g{g}_rms_f{i}")
            for ob in range(8):
                wg = WGATE2[ob % 2]
                kwg = kWGATE2[ob % 2]
                for h in range(2):
                    dma("sync",
                        wg[:, h * 4096:(h + 1) * 4096].rearrange("p (k o) -> p k o", o=512),
                        scr_win[h * 8:(h + 1) * 8, :, ob * 512:(ob + 1) * 512]
                        .rearrange("k p o -> p k o"),
                        [kT("scre", h * 8, (h + 1) * 8)],
                        [("AR", kwg[1] + h * 8192, kwg[1] + (h + 1) * 8192)],
                        tag=f"g{g}_wg{ob}_{h}",
                        stream=f"wg{ob % 2}{'b' if h else ''}")
                for i in range(NTT):
                    pb = pb4r[i]
                    pairs = [(XNT[k][:, i * 128:(i + 1) * 128], wg[:, k * 512:(k + 1) * 512])
                             for k in range(16)]
                    pairs.append((ones_bf[0:1, 0:128], b_bf[0:1, ob * 512:(ob + 1) * 512]))
                    mm(pb[:, :], pairs,
                       [kwg] + [("AR", kXNT[k][1] + i * 256, kXNT[k][1] + (i + 1) * 256)
                                for k in range(16)]
                       + [kT("onesb"), kT("b_bf")], kP4(i), tag=f"g{g}_gate{ob}_{i}")
                    gs, kgs = (gsb[:, :], kT("gsb")) if (ob * 4 + i) % 2 == 0 \
                        else (sqd[:, 512:1024], ("sqd", 512, 1024))
                    act(gs, pb[:, :], AF.Silu, [kP4(i)], [kgs],
                        tag=f"g{g}_gs{ob}_{i}")
                    ysl = Yt[i][:, ob * 512:(ob + 1) * 512]
                    kys = ("AR", kY[i][1] + ob * 1024, kY[i][1] + (ob + 1) * 1024)
                    tt("vector", ysl, ysl, gs, ALU.mult, [kys, kgs], [kys],
                       tag=f"g{g}_xg{ob}_{i}")
                    sc = (16 + i) if ob == 0 else (20 + i) if ob % 2 == 0 else (80 + i)
                    act(sqd[:, 0:512], ysl, AF.Square, [kys],
                        [("sqd", 0, 512), kT("statc", sc, sc + 1)],
                        accum_out=statc[:, sc:sc + 1], tag=f"g{g}_rms_a{i}_{ob}")
                    if ob > 0:
                        tt("vector", statc[:, 16 + i:17 + i], statc[:, 16 + i:17 + i],
                           statc[:, sc:sc + 1], ALU.add,
                           [kT("statc", 16 + i, 17 + i), kT("statc", sc, sc + 1)],
                           [kT("statc", 16 + i, 17 + i)], tag=f"g{g}_rms_c{i}_{ob}")
                    if ob == 7:
                        rms_fin(i)
            for i in range(NTT):
                for fbb in range(4):
                    f0 = fbb * 8
                    tr_batch8(
                        [Yt[i][:, (f0 + j) * 128:(f0 + j + 1) * 128] for j in range(8)],
                        ar3(kXGT[i][1] + f0 * 256, 256, 8, 128),
                        [kXGT[i]], [kY[i]] * 8, fbb % 2, tag=f"g{g}_xgt{i}_{fbb}")
            for h in range(2):
                dma("sync",
                    WFFN2[0][:, h * 4096:(h + 1) * 4096].rearrange("p (k o) -> p k o", o=512),
                    wdt_scr[0][h * 8:(h + 1) * 8].rearrange("k p o -> p k o"),
                    [kT("wdtscr", h * 8, (h + 1) * 8)],
                    [("AR", kWFFN2[0][1] + h * 8192, kWFFN2[0][1] + (h + 1) * 8192)],
                    tag=f"g{g}_fw0pre{h}", stream=f"wf0{'b' if h else ''}")
            for oq in range(4):
                wo = WOUT2[oq % 2]
                kwo = kWOUT2[oq % 2]
                for qh in range(4):
                    dma("sync",
                        wo[:, qh * 4096:(qh + 1) * 4096].rearrange("p (k o) -> p k o", o=512),
                        scr_wo[qh * 8:(qh + 1) * 8, :, oq * 512:(oq + 1) * 512]
                        .rearrange("k p o -> p k o"),
                        [kT("scro", qh * 8, (qh + 1) * 8)],
                        [("AR", kwo[1] + qh * 8192, kwo[1] + (qh + 1) * 8192)],
                        tag=f"g{g}_wo{oq}_{qh}",
                        stream=f"wo{oq % 2}{['', 'b', 'c', 'd'][qh]}")
                for i in range(NTT):
                    pb = pb4r[i]
                    pairs = [(XGT[i][:, fb * 128:(fb + 1) * 128], wo[:, fb * 512:(fb + 1) * 512])
                             for fb in range(32)]
                    pairs.append((ones_bf[0:1, 0:128], outb_bf[0:1, oq * 512:(oq + 1) * 512]))
                    mm(pb[:, :], pairs,
                       [kXGT[i], kT("onesb"), kT("outb"), kwo],
                       kP4(i), tag=f"g{g}_op{i}_{oq}")
                    tt("vector", xres_t[i][:, oq * 512:(oq + 1) * 512],
                       xres_t[i][:, oq * 512:(oq + 1) * 512], pb[:, :], ALU.add,
                       [kT("xres", i * D + oq * 512, i * D + (oq + 1) * 512), kP4(i)],
                       [kT("xres", i * D + oq * 512, i * D + (oq + 1) * 512)],
                       tag=f"g{g}_opev{i}_{oq}")
                    if oq == 3:
                        ln_stats(24 + i, 28 + i, 32 + i, 36 + i, xres_t[i], kxres[i],
                                 f"g{g}_ln2_{i}")

            if debug and g == 0:
                for i in range(4):
                    dma("sync", dbg["d_x1"][i * 128:(i + 1) * 128, :], xres_t[i],
                        [kxres[i]], [kT("dbg6")], tag=f"dx1_{i}", force_inc=True, stream="dbg")

            for i in range(NTT):
                ts("vector", X2N[:, :], xres_t[i], statc[:, 32 + i:33 + i],
                   statc[:, 36 + i:37 + i], ALU.add, ALU.mult,
                   [kxres[i], kT("statc", 32 + i, 33 + i), kT("statc", 36 + i, 37 + i)],
                   [kX2N], tag=f"g{g}_ln2ap{i}")
                for b in range(16):
                    vred(statc[:, 40 + b:41 + b], X2N[:, b * 128:(b + 1) * 128], True,
                         [kT("statc", 40, 56)], [kX2N], tag=f"g{g}_qmx{i}_{b}")
                ts("vector", statc[:, 40:56], statc[:, 40:56], 2.0 / 448.0, None, ALU.mult, None,
                   [kT("statc", 40, 56)], [kT("statc", 40, 56)], tag=f"g{g}_qs{i}")
                P.emit("vector", lambda v: v.reciprocal(out=statc[:, 64:80], in_=statc[:, 40:56]),
                       reads=[kT("statc", 40, 56)], writes=[kT("statc", 64, 80)], tag=f"g{g}_qsi{i}")
                for b in range(16):
                    kQ8b = ("AR", 90 * 1024 + b * 128, 90 * 1024 + (b + 1) * 128)
                    kXDQb = ("AR", 92 * 1024 + b * 256, 92 * 1024 + (b + 1) * 256)
                    ts("vector", Q8[:, b * 128:(b + 1) * 128], X2N[:, b * 128:(b + 1) * 128],
                       statc[:, 64 + b:65 + b], None, ALU.mult, None,
                       [kX2N, kT("statc", 64, 80)], [kQ8b], tag=f"g{g}_q{i}_{b}")
                    act(XDEQ[:, b * 128:(b + 1) * 128], Q8[:, b * 128:(b + 1) * 128],
                        AF.Copy, [kQ8b, kT("statc", 40, 56)], [kXDQb],
                        scale=statc[:, 40 + b:41 + b], tag=f"g{g}_dq{i}_{b}")
                if debug and g == 0 and i == 0:
                    dma("sync", dbg["d_x2n"], X2N[:, :], [kX2N], [kT("dbg7")],
                        tag="dx2n", force_inc=True, stream="dbg")
                    dma("gpsimd", dbg["d_xdeq"], XDEQ[:, :], [kXDEQ], [kT("dbg8")],
                        tag="dxdeq", force_inc=True, stream="dbg")
                for kb in range(2):
                    k0 = kb * 8
                    tr_batch8(
                        [XDEQ[:, (k0 + j) * 128:(k0 + j + 1) * 128] for j in range(8)],
                        ar3(kXDT[k0][1] + i * 256, 1024, 8, 128),
                        [("AR", kXDT[k0 + j][1] + i * 256,
                          kXDT[k0 + j][1] + (i + 1) * 256) for j in range(8)],
                        [("AR", 92 * 1024 + (k0 + j) * 256, 92 * 1024 + (k0 + j + 1) * 256)
                         for j in range(8)], (i * 2 + kb) % 2, tag=f"g{g}_fxt{i}_{kb}")
            for oh in range(4):
                wf = WFFN2[oh % 2]
                kwf = kWFFN2[oh % 2]
                if oh > 0:
                    for h in range(2):
                        dma("sync",
                            wf[:, h * 4096:(h + 1) * 4096].rearrange("p (k o) -> p k o", o=512),
                            wdt_scr[oh][h * 8:(h + 1) * 8].rearrange("k p o -> p k o"),
                            [kT("wdtscr", oh * 16 + h * 8, oh * 16 + (h + 1) * 8)],
                            [("AR", kwf[1] + h * 8192, kwf[1] + (h + 1) * 8192)],
                            tag=f"g{g}_fw{oh}_{h}", stream=f"wf{oh % 2}{'b' if h else ''}")
                if debug and g == 0 and oh == 0:
                    dma("gpsimd", dbg["d_wf"], WFFN2[0][:, 0:512], [kWFFN2[0]], [kT("dbg9")],
                        tag="dwf", force_inc=True, stream="dbg")
                for i in range(NTT):
                    pb = pb4r[i]
                    pairs = [(XDT[k][:, i * 128:(i + 1) * 128], wf[:, k * 512:(k + 1) * 512])
                             for k in range(16)]
                    mm(pb[:, :], pairs,
                       [kwf] + [("AR", kXDT[k][1] + i * 256, kXDT[k][1] + (i + 1) * 256)
                                for k in range(16)],
                       kP4(i), tag=f"g{g}_ffn{oh}_{i}")
                    tt("vector", xres_t[i][:, oh * 512:(oh + 1) * 512],
                       xres_t[i][:, oh * 512:(oh + 1) * 512], pb[:, :], ALU.add,
                       [kT("xres", i * D + oh * 512, i * D + (oh + 1) * 512), kP4(i)],
                       [kT("xres", i * D + oh * 512, i * D + (oh + 1) * 512)],
                       tag=f"g{g}_ffnev{i}_{oh}")
                    if oh == 3:
                        dma("sync", out_h[(g * NTT + i) * 128:(g * NTT + i + 1) * 128, :],
                            xres_t[i],
                            [kxres[i]], [kT("OUT", g * NTT + i, g * NTT + i + 1)],
                            tag=f"g{g}_st{i}", force_inc=True, stream=f"store{i}")
                        if g + 1 < NG:
                            dma("sync", xres_t[i],
                                x_in[((g + 1) * NTT + i) * 128:((g + 1) * NTT + i + 1) * 128, :],
                                [kxres[i]], [kxres[i]], tag=f"g{g + 1}_xld{i}",
                                stream=f"xload{i}")
                            ln1_tile(g + 1, i)

        P.finalize()
        assert len(P.streams) <= len(ssem_pool), P.streams
        ssems = {name: ssem_pool[i] for i, name in enumerate(P.streams)}
        final_waits = [(name, 16 * P.stream_count[name]) for name in P.streams
                       if name.startswith("store") or name == "dbg"]

        @block.sync
        def _(sync):
            P.run("sync", sync, sems, ssems)
            for name, v in final_waits:
                sync.wait_ge(ssems[name], v)

        @block.tensor
        def _(t):
            P.run("tensor", t, sems, ssems)

        @block.vector
        def _(v):
            P.run("vector", v, sems, ssems)

        @block.scalar
        def _(s):
            P.run("scalar", s, sems, ssems)

        @block.gpsimd
        def _(gp):
            P.run("gpsimd", gp, sems, ssems)

    return nc


_cached = {}


def _get_program(debug=False):
    key = bool(debug)
    if key not in _cached:
        _cached[key] = build_program(debug=key)
    return _cached[key]


def kernel(L0_attn_hidden_states, L0_attn_in_proj_w, L0_attn_in_proj_b,
           L0_attn_conv_w, L0_attn_dt_bias, L0_attn_A_log, L0_attn_D,
           L0_attn_norm_weight, L0_attn_out_proj_w, L0_attn_out_proj_b,
           L0_ffn_weight, _debug=False, _cores=None):
    hs = np.ascontiguousarray(np.asarray(L0_attn_hidden_states, dtype=np.float32))
    const = {
        "w_in": np.ascontiguousarray(np.asarray(L0_attn_in_proj_w, np.float32)),
        "b_in": np.ascontiguousarray(np.asarray(L0_attn_in_proj_b, np.float32).reshape(1, PROJ)),
        "conv_w": np.ascontiguousarray(np.asarray(L0_attn_conv_w, np.float32)),
        "dt_bias": np.ascontiguousarray(np.asarray(L0_attn_dt_bias, np.float32).reshape(NH, 1)),
        "a_log": np.ascontiguousarray(np.asarray(L0_attn_A_log, np.float32).reshape(NH, 1)),
        "d_head": np.ascontiguousarray(np.asarray(L0_attn_D, np.float32).reshape(1, NH)),
        "nw": np.ascontiguousarray(np.asarray(L0_attn_norm_weight, np.float32).reshape(1, GATE)),
        "w_out": np.ascontiguousarray(np.asarray(L0_attn_out_proj_w, np.float32)),
        "b_out": np.ascontiguousarray(np.asarray(L0_attn_out_proj_b, np.float32).reshape(1, D)),
        "w_ffn": np.ascontiguousarray(np.asarray(L0_ffn_weight, np.float32)),
        "eye_bf": np.eye(128, dtype=ml_dtypes.bfloat16),
        "eye32": np.eye(128, dtype=np.float32),
        "utri": np.triu(np.ones((128, 128), np.float32)),
        "trimask": np.concatenate(
            [np.triu(np.ones((128, 128), np.float32)),
             np.ones((128, 128), np.float32)], axis=1).astype(ml_dtypes.bfloat16),
    }
    ncores = hs.shape[0]
    nc = _get_program(debug=_debug)
    in_maps = [dict(const, x=np.ascontiguousarray(hs[i])) for i in range(ncores)]
    core_ids = list(range(ncores)) if _cores is None else _cores
    res = run_bass_kernel_spmd(nc, in_maps, core_ids=core_ids)
    out = np.stack([res.results[i]["out"] for i in range(len(core_ids))], axis=0)
    if _debug:
        return out, [res.results[i] for i in range(len(core_ids))]
    return out


# version 25
# speedup vs baseline: 1.0276x; 1.0276x over previous
import numpy as np
import ml_dtypes
import concourse.bass as bass
import concourse.mybir as mybir
from concourse.bass_utils import run_bass_kernel_spmd

F32 = mybir.dt.float32
BF16 = mybir.dt.bfloat16
FP8 = mybir.dt.float8e4
AF = mybir.ActivationFunctionType
ALU = mybir.AluOpType
AX = mybir.AxisListType

S, D = 2048, 2048
TG = 512
NG = S // TG
LC = 256
NCH = TG // LC
NH, HP, NST = 64, 64, 128
GATE = 4096
HBCF = 4352
PROJ = 8512
NTT = TG // 128
NFT = D // 128

ENGINES = ("sync", "tensor", "vector", "scalar", "gpsimd")


class Prog:

    def __init__(self):
        self.ops = {e: [] for e in ENGINES}
        self.writers = {}
        self.readers = {}
        self.stream_count = {}
        self.stream_last = {}

    @staticmethod
    def _ov(lo, hi, lo2, hi2):
        return lo < hi2 and lo2 < hi

    def emit(self, eng, fn, reads=(), writes=(), dma=False, force_inc=False, tag="",
             stream=None):
        if dma:
            stream = stream or (eng + "_misc")
        idx = len(self.ops[eng])
        deps = set()
        for (sp, lo, hi) in reads:
            for (lo2, hi2, src) in self.writers.get(sp, ()):
                if self._ov(lo, hi, lo2, hi2):
                    deps.add(src)
        for (sp, lo, hi) in writes:
            for (lo2, hi2, src) in self.writers.get(sp, ()):
                if self._ov(lo, hi, lo2, hi2):
                    deps.add(src)
            for (lo2, hi2, src) in self.readers.get(sp, ()):
                if self._ov(lo, hi, lo2, hi2):
                    deps.add(src)
        me = (eng, idx)
        deps.discard(me)
        if dma:
            prev = self.stream_last.get(stream)
            if prev is not None:
                deps.add(prev)
        snap = dict(self.stream_count)
        self.ops[eng].append(dict(eng=eng, idx=idx, fn=fn, deps=deps, dma=dma,
                                  tag=tag, needs_inc=force_inc, stream=stream,
                                  snap=snap))
        if dma:
            self.stream_count[stream] = self.stream_count.get(stream, 0) + 1
            self.stream_last[stream] = me
        for (sp, lo, hi) in writes:
            self.writers[sp] = [e for e in self.writers.get(sp, ())
                                if not (lo <= e[0] and e[1] <= hi)]
            self.writers[sp].append((lo, hi, me))
            self.readers[sp] = [e for e in self.readers.get(sp, ())
                                if not (lo <= e[0] and e[1] <= hi)]
        for (sp, lo, hi) in reads:
            self.readers.setdefault(sp, []).append((lo, hi, me))

    def finalize(self):
        for e in ENGINES:
            for op in self.ops[e]:
                for (se, si) in op["deps"]:
                    self.ops[se][si]["needs_inc"] = True
        for e in ENGINES:
            for op in self.ops[e]:
                if op["dma"]:
                    op["needs_inc"] = True
        self.cum = {}
        for e in ENGINES:
            c, cl = 0, []
            for op in self.ops[e]:
                if op["needs_inc"] and not op["dma"]:
                    c += 1
                cl.append(c)
            self.cum[e] = cl
        self.streams = sorted(self.stream_count)
        for e in ENGINES:
            vc = {}
            for op in self.ops[e]:
                req = {}
                for (se, si) in op["deps"]:
                    p = self.ops[se][si]
                    if p["dma"]:
                        st = p["stream"]
                        v = 16 * op["snap"].get(st, 0)
                        key = ("S", st)
                    else:
                        v = self.cum[se][si]
                        key = ("E", se)
                    if v > req.get(key, 0):
                        req[key] = v
                waits = []
                for key, v in req.items():
                    if v > vc.get(key, 0):
                        waits.append((key, v))
                        vc[key] = v
                op["waits"] = waits

    def run(self, eng_name, eng, sems, ssems):
        for op in self.ops[eng_name]:
            for ((kind, name), v) in op["waits"]:
                eng.wait_ge(sems[name] if kind == "E" else ssems[name], v)
            instr = op["fn"](eng)
            if op["needs_inc"]:
                assert instr is not None, f"needs inc but fn returned None: {op['tag']}"
                if op["dma"]:
                    instr.then_inc(ssems[op["stream"]], 16)
                else:
                    instr.then_inc(sems[eng_name], 1)


def build_program(debug=False):
    nc = bass.Bass("TRN2", target_bir_lowering=False, debug=False)

    x_in = nc.dram_tensor("x", [S, D], F32, kind="ExternalInput").ap()
    w_in = nc.dram_tensor("w_in", [D, PROJ], F32, kind="ExternalInput").ap()
    b_in = nc.dram_tensor("b_in", [1, PROJ], F32, kind="ExternalInput").ap()
    convw_h = nc.dram_tensor("conv_w", [HBCF, 4], F32, kind="ExternalInput").ap()
    dtb_h = nc.dram_tensor("dt_bias", [NH, 1], F32, kind="ExternalInput").ap()
    alog_h = nc.dram_tensor("a_log", [NH, 1], F32, kind="ExternalInput").ap()
    drow_h = nc.dram_tensor("d_head", [1, NH], F32, kind="ExternalInput").ap()
    nw_h = nc.dram_tensor("nw", [1, GATE], F32, kind="ExternalInput").ap()
    w_out = nc.dram_tensor("w_out", [GATE, D], F32, kind="ExternalInput").ap()
    b_out = nc.dram_tensor("b_out", [1, D], F32, kind="ExternalInput").ap()
    w_ffn = nc.dram_tensor("w_ffn", [D, D], F32, kind="ExternalInput").ap()
    eye_bf_h = nc.dram_tensor("eye_bf", [128, 128], BF16, kind="ExternalInput").ap()
    eye32_h = nc.dram_tensor("eye32", [128, 128], F32, kind="ExternalInput").ap()
    utri_h = nc.dram_tensor("utri", [128, 128], F32, kind="ExternalInput").ap()
    trim_h = nc.dram_tensor("trimask", [128, 256], BF16, kind="ExternalInput").ap()
    out_h = nc.dram_tensor("out", [S, D], F32, kind="ExternalOutput").ap()
    wdt_scr = nc.dram_tensor("wdt_scr", [4, 16, 128, 512], BF16, kind="Internal").ap()
    scr_win = nc.dram_tensor("scr_win", [16, 128, PROJ], BF16, kind="Internal").ap()
    scr_wo = nc.dram_tensor("scr_wo", [32, 128, 2048], BF16, kind="Internal").ap()
    dbg = {}
    if debug:
        for nm, shp in [("d_xbc", [HBCF, TG]), ("d_dt", [NH, TG]), ("d_acum", [NH, TG]),
                        ("d_y", [4 * 128, GATE]), ("d_x1", [TG, D]), ("d_xn", [TG, D]),
                        ("d_x2n", [128, D]), ("d_xdeq", [128, D]), ("d_wf", [128, 512])]:
            dbg[nm] = nc.dram_tensor(nm, shp, F32, kind="ExternalOutput").ap()

    P = Prog()

    import contextlib
    with contextlib.ExitStack() as _st:
        xres = _st.enter_context(nc.sbuf_tensor("xres", [128, 4 * D], F32))
        carry = _st.enter_context(nc.sbuf_tensor("carry", [128, NH * HP], BF16))
        tails = _st.enter_context(nc.sbuf_tensor("tails", [128, 34 * 3], BF16))
        convw = _st.enter_context(nc.sbuf_tensor("convw", [128, 34 * 4], F32))
        dtb = _st.enter_context(nc.sbuf_tensor("dtb", [NH, 1], F32))
        alog = _st.enter_context(nc.sbuf_tensor("alog", [NH, 1], F32))
        expA = _st.enter_context(nc.sbuf_tensor("expA", [NH, 1], F32))
        drow_s = _st.enter_context(nc.sbuf_tensor("drow_s", [1, NH], F32))
        drep = _st.enter_context(nc.sbuf_tensor("drep", [128, NH], F32))
        dtraw = _st.enter_context(nc.sbuf_tensor("dtraw", [NH, TG], F32))
        sp1 = _st.enter_context(nc.sbuf_tensor("sp1", [NH, TG], F32))
        dtv = _st.enter_context(nc.sbuf_tensor("dtv", [NH, TG], F32))
        abt = _st.enter_context(nc.sbuf_tensor("abt", [NH, TG], F32))
        acum = _st.enter_context(nc.sbuf_tensor("acum", [NH, TG], F32))
        decf = _st.enter_context(nc.sbuf_tensor("decf", [NH, LC], F32))
        decf2 = _st.enter_context(nc.sbuf_tensor("decf2", [NH, LC], F32))
        ddf = _st.enter_context(nc.sbuf_tensor("ddf", [NH, LC], F32))
        acT = _st.enter_context(nc.sbuf_tensor("acT", [128, 128], F32))
        acumT = _st.enter_context(nc.sbuf_tensor("acumT", [128, 128], F32))
        nacT = _st.enter_context(nc.sbuf_tensor("nacT", [128, 128], F32))
        sdT = _st.enter_context(nc.sbuf_tensor("sdT", [128, 128], F32))
        dtT = _st.enter_context(nc.sbuf_tensor("dtT", [128, 128], F32))
        ddT = _st.enter_context(nc.sbuf_tensor("ddT", [128, 128], F32))
        exptot = _st.enter_context(nc.sbuf_tensor("exptot", [128, NH], F32))
        alastr = _st.enter_context(nc.sbuf_tensor("alastr", [1, NH], F32))
        statc = _st.enter_context(nc.sbuf_tensor("statc", [128, 84], F32))
        sqd = _st.enter_context(nc.sbuf_tensor("sqd", [128, 1024], BF16))

        ksqd = lambda: ("sqd", 0, 1 << 30)
        constc = _st.enter_context(nc.sbuf_tensor("constc", [128, 2], F32))
        nwrep = _st.enter_context(nc.sbuf_tensor("nwrep", [128, GATE], BF16))
        b_bf = _st.enter_context(nc.sbuf_tensor("b_bf", [1, PROJ], BF16))
        outb_bf = _st.enter_context(nc.sbuf_tensor("outb_bf", [1, D], BF16))
        ones_bf = _st.enter_context(nc.sbuf_tensor("ones_bf", [1, 512], BF16))
        ones3bf = _st.enter_context(nc.sbuf_tensor("ones3bf", [3, 128], BF16))
        ones32 = _st.enter_context(nc.sbuf_tensor("ones32", [1, 128], F32))
        ones128 = _st.enter_context(nc.sbuf_tensor("ones128", [128, 128], F32))
        eye_bf = _st.enter_context(nc.sbuf_tensor("eye_bf_s", [128, 128], BF16))
        eye32 = _st.enter_context(nc.sbuf_tensor("eye32_s", [128, 128], F32))
        utri = _st.enter_context(nc.sbuf_tensor("utri_s", [128, 128], F32))
        trim = _st.enter_context(nc.sbuf_tensor("trim", [128, 256], BF16))
        gsb = _st.enter_context(nc.sbuf_tensor("gsb", [128, 512], BF16))
        AR = _st.enter_context(nc.sbuf_tensor("arena", [128, 114 * 1024], mybir.dt.uint8))
        pb0 = _st.enter_context(nc.psum_tensor("pb0", [128, 512], F32))
        pb1 = _st.enter_context(nc.psum_tensor("pb1", [128, 512], F32))
        pb2 = _st.enter_context(nc.psum_tensor("pb2", [128, 512], F32))
        pb3 = _st.enter_context(nc.psum_tensor("pb3", [128, 512], F32))
        pb4 = _st.enter_context(nc.psum_tensor("pb4", [128, 512], F32))
        pb5 = _st.enter_context(nc.psum_tensor("pb5", [128, 512], F32))
        pb6 = _st.enter_context(nc.psum_tensor("pb6", [128, 512], F32))
        pb7 = _st.enter_context(nc.psum_tensor("pb7", [128, 512], F32))
        s_sync = _st.enter_context(nc.semaphore("s_sync"))
        s_tensor = _st.enter_context(nc.semaphore("s_tensor"))
        s_vector = _st.enter_context(nc.semaphore("s_vector"))
        s_scalar = _st.enter_context(nc.semaphore("s_scalar"))
        s_gpsimd = _st.enter_context(nc.semaphore("s_gpsimd"))
        ssem_pool = [_st.enter_context(nc.semaphore(f"s_dq{i}")) for i in range(40)]
        block = _st.enter_context(nc.Block())
        sems = {"sync": s_sync, "tensor": s_tensor, "vector": s_vector,
                "scalar": s_scalar, "gpsimd": s_gpsimd}
        pbank = [pb0, pb1, pb2, pb3, pb4, pb5, pb6, pb7]
        pb4r = [pb0, pb1, pb4, pb5]
        def kP4(j):
            return (f"P{[0, 1, 4, 5][j]}", 0, 512)

        def arv(off_kb, nbytes, dt_):
            return AR[0:128, off_kb * 1024: off_kb * 1024 + nbytes].bitcast(dt_)
        def kAR(off_kb, nbytes):
            return ("AR", off_kb * 1024, off_kb * 1024 + nbytes)

        XBC = [arv(i, 1024, BF16) for i in range(34)]
        XNT = [arv(34 + i, 1024, BF16) for i in range(16)]
        Yt = [arv(50 + 8 * i, 8192, BF16) for i in range(4)]
        Xt = [arv(82 + 8 * i, 8192, BF16) for i in range(2)]
        AROW2 = [AR[0:3, 98 * 1024: 98 * 1024 + 4 * 1024].bitcast(BF16),
                 AR[0:3, 106 * 1024: 106 * 1024 + 4 * 1024].bitcast(BF16)]
        WGATE2 = [arv(16, 16384, BF16), arv(0, 16384, BF16)]
        kWGATE2 = [kAR(16, 16384), kAR(0, 16384)]
        XGT = [arv(8 * i, 8192, BF16) for i in range(4)]
        X2N = arv(82, 8192, F32)
        Q8 = arv(90, 2048, FP8)
        XDEQ = arv(92, 4096, BF16)
        XDT = [arv(16 + i, 1024, BF16) for i in range(16)]
        WOUT2 = [arv(82, 32768, BF16), arv(50, 32768, BF16)]
        kWOUT2 = [kAR(82, 32768), kAR(50, 32768)]
        WFFN2 = [arv(34, 16384, BF16), arv(0, 16384, BF16)]
        kWFFN2 = [kAR(34, 16384), kAR(0, 16384)]
        WFASM = [arv(50 + i, 1024, BF16) for i in range(16)]
        NWROW = AR[0:1, 98 * 1024: 98 * 1024 + GATE * 4].bitcast(F32)
        WEAGER = [arv(82, 16384, BF16), arv(98, 16384, BF16)]
        kWEAGER = [kAR(82, 16384), kAR(98, 16384)]

        def arbuf(off, nbytes, dt_):
            esz = {F32: 4, BF16: 2, FP8: 1}[dt_]
            view = AR[0:128, off: off + nbytes].bitcast(dt_)
            def key(lo=0, hi=None):
                h = nbytes // esz if hi is None else hi
                return ("AR", off + lo * esz, off + h * esz)
            return view, key

        xn1, kxn1 = arbuf(4096, 4096, BF16)
        xn1c = arv(50, 4096, BF16)
        xn1b = arv(86, 4096, BF16)
        X2Nb = arv(50, 8192, F32)
        Q8b = arv(58, 2048, FP8)
        XDEQb = arv(60, 4096, BF16)
        preE, kpreE = arbuf(34 * 1024, 3072, F32)
        Et, kEt = arbuf(34 * 1024 + 3072, 1536, BF16)
        Mt, kMt = arbuf(34 * 1024 + 4608, 1536, BF16)
        Gt, kGt = arbuf(34 * 1024 + 6144, 768, BF16)
        Bt, kBt = arbuf(34 * 1024 + 6912, 512, BF16)
        xd2, kxd2 = arbuf(34 * 1024 + 7424, 512, BF16)
        asp1f, kasp1 = arbuf(42 * 1024, 512, BF16)
        asp2f, kasp2 = arbuf(42 * 1024 + 512, 512, BF16)
        asp3f, kasp3 = arbuf(43 * 1024, 512, BF16)
        asprf, kaspr = arbuf(43 * 1024 + 1024, 1024, F32)
        asp1 = asp1f[0:NH, :]
        asp2 = asp2f[0:NH, :]
        asp3 = asp3f[0:NH, :]
        aspr = asprf[0:NH, :]
        pc2l = [arbuf(56 * 1024, 1040, BF16), arbuf(58 * 1024, 1040, BF16)]
        cacc2l = [arbuf(60 * 1024, 1024, BF16), arbuf(62 * 1024, 1024, BF16)]

        kXBC = [kAR(i, 1024) for i in range(34)]
        kXNT = [kAR(34 + i, 1024) for i in range(16)]
        kY = [kAR(50 + 8 * i, 8192) for i in range(4)]
        kXt = [kAR(82 + 8 * i, 8192) for i in range(2)]
        kAROW2 = [kAR(98, 8 * 1024), kAR(106, 8 * 1024)]
        kXGT = [kAR(8 * i, 8192) for i in range(4)]
        kX2N = kAR(82, 8192)
        kQ8 = kAR(90, 2048)
        kXDEQ = kAR(92, 4096)
        kXDT = [kAR(16 + i, 1024) for i in range(16)]
        kWFASM = [kAR(50 + i, 1024) for i in range(16)]

        ARBF = AR[0:128, :].bitcast(BF16)

        def ar3(off_bytes, stride_bytes, n_mid, n_inner):
            return bass.AP(tensor=ARBF.tensor, offset=off_bytes // 2,
                           ap=[[ARBF.ap[0][0], 128], [stride_bytes // 2, n_mid],
                               [1, n_inner]])

        def tr_batch8(srcs, dst3d, wkeys, rkeys, bank_sel, tag):
            bank = pb2 if bank_sel == 0 else pb3
            kb = kP(2) if bank_sel == 0 else kP(3)
            for j, (src_ap, rk) in enumerate(zip(srcs, rkeys)):
                pslot = bank.bitcast(BF16)[:, j * 128:(j + 1) * 128]
                transp(pslot, src_ap, eye_bf[:], [rk], kb, tag=f"{tag}_t{j}")
            act(dst3d, bank.bitcast(BF16)[:, 0:1024].rearrange("p (a b) -> p a b", a=8),
                AF.Copy, [kb], wkeys, tag=f"{tag}_e")

        def kP(i, lo=0, hi=512):
            return (f"P{i}", 0, 512)

        def kT(name, lo=0, hi=1 << 30):
            return (name, lo, hi)

        def dma(eng, out, in_, reads, writes, tag="", force_inc=False, stream=None):
            P.emit(eng, lambda e, out=out, in_=in_: e.dma_start(out=out, in_=in_),
                   reads=reads, writes=writes, dma=True, force_inc=force_inc, tag=tag,
                   stream=stream)

        def mm(psum_ap, pairs, reads, wkey, tag=""):
            def fn(t, psum_ap=psum_ap, pairs=pairs):
                n = len(pairs)
                ins = None
                for i, (l, r) in enumerate(pairs):
                    ins = t.matmul(psum_ap, l, r, start=(i == 0), stop=(i == n - 1))
                return ins
            P.emit("tensor", fn, reads=reads, writes=[wkey], tag=tag)

        def transp(psum_ap, in_ap, ident, reads, wkey, tag=""):
            P.emit("tensor", lambda t, a=psum_ap, b=in_ap, c=ident: t.transpose(a, b, c),
                   reads=list(reads) + [kT("eye")], writes=[wkey], tag=tag)

        def act(out, in_, func, reads, writes, bias=0.0, scale=1.0, accum_out=None, tag=""):
            if accum_out is None:
                P.emit("scalar",
                       lambda s, out=out, in_=in_, func=func, bias=bias, scale=scale:
                       s.activation(out=out, in_=in_, func=func, bias=bias, scale=scale),
                       reads=reads, writes=writes, tag=tag)
            else:
                def fn(s, out=out, in_=in_, func=func, bias=bias, scale=scale, ao=accum_out):
                    s.activation(out=out, in_=in_, func=func, bias=bias, scale=scale,
                                 accum_out=ao)
                    return s.drain()
                P.emit("scalar", fn, reads=reads, writes=writes, tag=tag)

        def ts(eng, out, in0, s1, s2, op0, op1, reads, writes, tag=""):
            if s2 is None and op1 is None:
                P.emit(eng, lambda e, out=out, in0=in0, s1=s1, op0=op0:
                       e.tensor_scalar(out=out, in0=in0, scalar1=s1, scalar2=None, op0=op0),
                       reads=reads, writes=writes, tag=tag)
            else:
                P.emit(eng, lambda e, out=out, in0=in0, s1=s1, s2=s2, op0=op0, op1=op1:
                       e.tensor_scalar(out=out, in0=in0, scalar1=s1, scalar2=s2, op0=op0, op1=op1),
                       reads=reads, writes=writes, tag=tag)

        def stt(out, in0, sc, in1, op0, op1, reads, writes, tag=""):
            P.emit("vector",
                   lambda v, out=out, in0=in0, sc=sc, in1=in1, op0=op0, op1=op1:
                   v.scalar_tensor_tensor(out=out, in0=in0, scalar=sc, in1=in1, op0=op0, op1=op1),
                   reads=reads, writes=writes, tag=tag)

        def tt(eng, out, in0, in1, op, reads, writes, tag=""):
            P.emit(eng,
                   lambda e, out=out, in0=in0, in1=in1, op=op:
                   e.tensor_tensor(out=out, in0=in0, in1=in1, op=op),
                   reads=reads, writes=writes, tag=tag)

        def cp(eng, out, in_, reads, writes, tag=""):
            P.emit(eng, lambda e, out=out, in_=in_: e.tensor_copy(out=out, in_=in_),
                   reads=reads, writes=writes, tag=tag)

        def vred(out, in_, op_abs, wkeys, rkeys, tag=""):
            P.emit("vector",
                   lambda v, out=out, in_=in_: (
                       v.reduce_max(out=out, in_=in_, axis=AX.X, apply_absolute_value=True)
                       if op_abs else v.reduce_sum(out=out, in_=in_, axis=AX.X)),
                   reads=rkeys, writes=wkeys, tag=tag)

        dma("sync", eye_bf[:], eye_bf_h, [], [kT("eye")], tag="eye")
        dma("sync", eye32[:], eye32_h, [], [kT("eye")], tag="eye32")
        dma("sync", utri[:], utri_h, [], [kT("utri")], tag="utri")
        dma("sync", trim[:], trim_h, [], [kT("trim")], tag="trim")
        dma("sync", convw[0:128, :].rearrange("p (i k) -> p i k", k=4),
            convw_h.rearrange("(i p) k -> p i k", p=128), [], [kT("convw")], tag="convw")
        dma("sync", dtb[:], dtb_h, [], [kT("dtb")], tag="dtb")
        dma("sync", alog[:], alog_h, [], [kT("alog")], tag="alog")
        dma("sync", drow_s[:], drow_h, [], [kT("drow")], tag="drow")
        P.emit("gpsimd", lambda g: g.memset(ones_bf[:], 1.0), writes=[kT("onesb")], tag="ones_bf")
        P.emit("gpsimd", lambda g: g.memset(ones3bf[:], 1.0), writes=[kT("ones3")], tag="ones3")
        P.emit("gpsimd", lambda g: g.memset(ones32[:], 1.0), writes=[kT("ones32")], tag="ones32")
        P.emit("gpsimd", lambda g: g.memset(ones128[:], 1.0), writes=[kT("ones128")], tag="ones128")
        P.emit("gpsimd", lambda g: g.memset(constc[:, 0:1], 1e-5), writes=[kT("constc")], tag="eps")
        P.emit("gpsimd", lambda g: g.memset(constc[:, 1:2], 1.0), writes=[kT("constc")], tag="one")
        P.emit("gpsimd", lambda g: g.memset(carry[:], 0.0), writes=[kT("carry", 0, NH * HP)], tag="carry0")
        P.emit("gpsimd", lambda g: g.memset(tails[:], 0.0), writes=[kT("tails", 0, 102)], tag="tails0")
        dma("gpsimd", b_bf[:], b_in, [], [kT("b_bf")], tag="b_bf")
        dma("gpsimd", outb_bf[:], b_out, [], [kT("outb")], tag="outb")
        act(expA[:], alog[:], AF.Exp, [kT("alog")], [kT("expA")], tag="expA")
        mm(pb6[:, 0:NH], [(ones32[:], drow_s[0:1, :])],
           [kT("ones32"), kT("drow")], kP(6, 0, NH), tag="drep_mm")
        act(drep[:], pb6[:, 0:NH], AF.Copy, [kP(6, 0, NH)], [kT("drep")], tag="drep_ev")
        dma("sync", NWROW, nw_h, [], [kAR(98, 16 * 1024)], tag="nwrow")
        for j in range(8):
            mm(pb6[:, 0:512], [(ones32[:], NWROW[0:1, j * 512:(j + 1) * 512])],
               [kT("ones32"), kAR(98, 16 * 1024)], kP(6), tag=f"nwrep{j}")
            act(nwrep[:, j * 512:(j + 1) * 512], pb6[:, 0:512], AF.Copy,
                [kP(6)], [kT("nwrep", j * 512, (j + 1) * 512)], tag=f"nwrep_ev{j}")

        xres_t = [xres[:, i * D:(i + 1) * D] for i in range(NTT)]
        kxres = [kT("xres", i * D, (i + 1) * D) for i in range(NTT)]

        def emit_xload(g):
            for i in range(NTT):
                dma("sync", xres_t[i], x_in[(g * NTT + i) * 128:(g * NTT + i + 1) * 128, :],
                    [], [kxres[i]], tag=f"g{g}_xld{i}", stream=f"xload{i}")

        def ln_stats(sxo, ssqo, nmo, rso, src, ksrc, tag):
            vred(statc[:, sxo:sxo + 1], src, False,
                 [kT("statc", sxo, sxo + 1)], [ksrc], tag=tag + "_sx")
            act(sqd[:], src[:, 0:1024], AF.Square, [ksrc],
                [ksqd(), kT("statc", ssqo, ssqo + 1)],
                accum_out=statc[:, ssqo:ssqo + 1], tag=tag + "_ssqa")
            act(sqd[:], src[:, 1024:2048], AF.Square, [ksrc],
                [ksqd(), kT("statc", 80, 81)],
                accum_out=statc[:, 80:81], tag=tag + "_ssqb")
            tt("vector", statc[:, ssqo:ssqo + 1], statc[:, ssqo:ssqo + 1],
               statc[:, 80:81], ALU.add,
               [kT("statc", ssqo, ssqo + 1), kT("statc", 80, 81)],
               [kT("statc", ssqo, ssqo + 1)], tag=tag + "_ssqc")
            ts("vector", statc[:, nmo:nmo + 1], statc[:, sxo:sxo + 1], -1.0 / D, None,
               ALU.mult, None, [kT("statc", sxo, sxo + 1)], [kT("statc", nmo, nmo + 1)],
               tag=tag + "_nm")
            ts("vector", statc[:, rso:rso + 1], statc[:, ssqo:ssqo + 1], 1.0 / D, None,
               ALU.mult, None, [kT("statc", ssqo, ssqo + 1)], [kT("statc", rso, rso + 1)],
               tag=tag + "_m2")
            stt(statc[:, rso:rso + 1], statc[:, nmo:nmo + 1], statc[:, nmo:nmo + 1],
                statc[:, rso:rso + 1], ALU.mult, ALU.subtract,
                [kT("statc", nmo, nmo + 1), kT("statc", rso, rso + 1)],
                [kT("statc", rso, rso + 1)], tag=tag + "_var")
            act(statc[:, rso:rso + 1], statc[:, rso:rso + 1], AF.Sqrt,
                [kT("statc", rso, rso + 1), kT("constc")], [kT("statc", rso, rso + 1)],
                bias=constc[:, 0:1], scale=-1.0, tag=tag + "_sqrt")
            P.emit("vector", lambda v, o=statc[:, rso:rso + 1]: v.reciprocal(out=o, in_=o),
                   reads=[kT("statc", rso, rso + 1)], writes=[kT("statc", rso, rso + 1)],
                   tag=tag + "_rc")

        def ln1_tile(g, i):
            ln_stats(i, 4 + i, 8 + i, 12 + i, xres_t[i], kxres[i], f"g{g}_ln1_{i}")

        def emit_ln1(g, stats=True):
            if stats:
                for i in range(NTT):
                    ln1_tile(g, i)
            for i in range(NTT):
                xi, kxi = (xn1, kxn1()) if i % 2 == 0 else (xn1c, kAR(50, 4096))
                ts("vector", xi[:], xres_t[i], statc[:, 8 + i:9 + i], statc[:, 12 + i:13 + i],
                   ALU.add, ALU.mult,
                   [kxres[i], kT("statc", 8 + i, 9 + i), kT("statc", 12 + i, 13 + i)],
                   [kxi], tag=f"g{g}_ln1ap{i}")
                for kb in range(2):
                    k0 = kb * 8
                    tr_batch8(
                        [xi[:, (k0 + j) * 128:(k0 + j + 1) * 128] for j in range(8)],
                        ar3(kXNT[k0][1] + i * 256, 1024, 8, 128),
                        [("AR", kXNT[k0 + j][1] + i * 256,
                          kXNT[k0 + j][1] + (i + 1) * 256) for j in range(8)],
                        [kxi] * 8, (i * 2 + kb) % 2, tag=f"g{g}_xnT{i}_{kb}")
                if debug and g == 0:
                    dma("gpsimd", dbg["d_xn"][i * 128:(i + 1) * 128, :], xi[:],
                        [kxi], [kT("dbgxn")], tag=f"dxn{i}", force_inc=True, stream="dbg")


        emit_xload(0)
        emit_ln1(0)


        for g in range(NG):
            if g > 0:
                emit_ln1(g, stats=False)

            for pi in range(35):
                wb = pi // 4
                wbuf = WEAGER[wb % 2]
                kwbuf = kWEAGER[wb % 2]
                if pi % 4 == 0:
                    bw = min(512, PROJ - GATE - wb * 512)
                    for h in range(2):
                        kwh = ("AR", kwbuf[1] + h * 8192, kwbuf[1] + (h + 1) * 8192)
                        if g == 0:
                            dma("gpsimd",
                                wbuf[:, h * 4096:(h + 1) * 4096]
                                .rearrange("p (k o) -> p k o", o=512)[:, :, 0:bw],
                                w_in[h * 1024:(h + 1) * 1024,
                                     GATE + wb * 512: GATE + wb * 512 + bw]
                                .rearrange("(k p) o -> p k o", p=128),
                                [], [kwh], tag=f"g{g}_wldd{wb}_{h}",
                                stream=f"we{wb % 2}{'b' if h else ''}")
                        else:
                            dma("sync",
                                wbuf[:, h * 4096:(h + 1) * 4096]
                                .rearrange("p (k o) -> p k o", o=512)[:, :, 0:bw],
                                scr_win[h * 8:(h + 1) * 8, :,
                                        GATE + wb * 512: GATE + wb * 512 + bw]
                                .rearrange("k p o -> p k o"),
                                [kT("scre", h * 8, (h + 1) * 8)], [kwh],
                                tag=f"g{g}_wld{wb}_{h}",
                                stream=f"we{wb % 2}{'b' if h else ''}")
                width = 128 if pi < 34 else 64
                col0 = GATE + pi * 128
                o0 = (pi % 4) * 128
                pb = pb4r[pi % 4]
                pairs = [(wbuf[:, k * 512 + o0: k * 512 + o0 + width], XNT[k][:, :])
                         for k in range(16)]
                pairs.append((b_bf[0:1, col0:col0 + width], ones_bf[0:1, 0:512]))
                mm(pb[0:width, :], pairs,
                   [kT("b_bf"), kT("onesb"), kwbuf] + [kXNT[k] for k in range(16)],
                   kP4(pi % 4), tag=f"g{g}_proj{pi}")
                if pi < 34:
                    pc, kpc = pc2l[pi % 2]
                    cacc, kcacc = cacc2l[pi % 2]
                    cp("vector", pc[:, 0:3], tails[:, pi * 3:(pi + 1) * 3],
                       [kT("tails", pi * 3, (pi + 1) * 3)], [kpc(0, 3)], tag=f"g{g}_tin{pi}")
                    act(pc[:, 3:515], pb[:, 0:512], AF.Copy, [kP4(pi % 4)], [kpc(3, 515)],
                        tag=f"g{g}_pcev{pi}")
                    cp("vector", tails[:, pi * 3:(pi + 1) * 3], pc[:, 512:515],
                       [kpc(3, 515)], [kT("tails", pi * 3, (pi + 1) * 3)], tag=f"g{g}_tou{pi}")
                    ts("vector", cacc[:], pc[:, 0:512], convw[0:128, pi * 4:pi * 4 + 1], None,
                       ALU.mult, None, [kpc(0, 515), kT("convw")], [kcacc()],
                       tag=f"g{g}_cv0_{pi}")
                    for kk in range(1, 4):
                        stt(cacc[:], pc[:, kk:512 + kk], convw[0:128, pi * 4 + kk:pi * 4 + kk + 1],
                            cacc[:], ALU.mult, ALU.add,
                            [kpc(0, 515), kT("convw"), kcacc()], [kcacc()],
                            tag=f"g{g}_cv{kk}_{pi}")
                    act(XBC[pi][:, :], cacc[:], AF.Silu, [kcacc()], [kXBC[pi]],
                        tag=f"g{g}_silu{pi}")
                else:
                    act(dtraw[:, :], pb[0:NH, :], AF.Copy, [kP4(pi % 4)], [kT("dtraw")],
                        tag=f"g{g}_dtev")

            if debug and g == 0:
                for pi in range(34):
                    dma("gpsimd", dbg["d_xbc"][pi * 128:(pi + 1) * 128, :], XBC[pi][:, :],
                        [kXBC[pi]], [kT("dbg1")], tag=f"dxbc{pi}", force_inc=True, stream="dbg")

            if g == 0:
                for k in range(16):
                    dma("gpsimd", scr_win[k], w_in[k * 128:(k + 1) * 128, :],
                        [], [kT("scre", k, k + 1)], tag=f"wp_{k}", stream=f"wprep{k % 4}")
                for k2 in range(32):
                    dma("gpsimd", scr_wo[k2], w_out[k2 * 128:(k2 + 1) * 128, :],
                        [], [kT("scro", k2, k2 + 1)], tag=f"wpo_{k2}", stream=f"wprep{k2 % 4}")
                for oh in range(4):
                    for j in range(4):
                        i = oh * 4 + j
                        dma("sync", X2N[:, :], w_ffn[i * 128:(i + 1) * 128, :],
                            [], [kX2N], tag=f"wf_ld{i}")
                        for b in range(16):
                            vred(statc[:, 48 + b:49 + b], X2N[:, b * 128:(b + 1) * 128], True,
                                 [kT("statc", 48, 64)], [kX2N], tag=f"wf_mx{i}_{b}")
                        ts("vector", statc[:, 48:64], statc[:, 48:64], 2.0 / 448.0, None, ALU.mult, None,
                           [kT("statc", 48, 64)], [kT("statc", 48, 64)], tag=f"wf_s{i}")
                        P.emit("vector", lambda v: v.reciprocal(out=statc[:, 64:80], in_=statc[:, 48:64]),
                               reads=[kT("statc", 48, 64)], writes=[kT("statc", 64, 80)], tag=f"wf_si{i}")
                        for b in range(16):
                            kQ8b = ("AR", 90 * 1024 + b * 128, 90 * 1024 + (b + 1) * 128)
                            kXDQb = ("AR", 92 * 1024 + b * 256, 92 * 1024 + (b + 1) * 256)
                            ts("vector", Q8[:, b * 128:(b + 1) * 128], X2N[:, b * 128:(b + 1) * 128],
                               statc[:, 64 + b:65 + b], None, ALU.mult, None,
                               [kX2N, kT("statc", 64, 80)], [kQ8b], tag=f"wf_q{i}_{b}")
                            act(XDEQ[:, b * 128:(b + 1) * 128], Q8[:, b * 128:(b + 1) * 128],
                                AF.Copy, [kQ8b, kT("statc", 48, 64)], [kXDQb],
                                scale=statc[:, 48 + b:49 + b], tag=f"wf_dq{i}_{b}")
                        for k in range(16):
                            pslot = pb2.bitcast(BF16)[:, (k % 8) * 128:(k % 8 + 1) * 128]
                            transp(pslot, XDEQ[:, k * 128:(k + 1) * 128], eye_bf[:],
                                   [("AR", 92 * 1024 + k * 256, 92 * 1024 + (k + 1) * 256)],
                                   kP(2, (k % 8) * 64, (k % 8) * 64 + 64), tag=f"wf_tr{i}_{k}")
                            act(WFASM[k][:, j * 128:(j + 1) * 128], pslot, AF.Copy,
                                [kP(2, (k % 8) * 64, (k % 8) * 64 + 64)], [kWFASM[k]], tag=f"wf_ev{i}_{k}")
                    for k in range(16):
                        dma("gpsimd", wdt_scr[oh, k], WFASM[k][:, :],
                            [kWFASM[k]], [kT("wdtscr", oh * 16 + k, oh * 16 + k + 1)], tag=f"wf_st{oh}_{k}", stream=f"scr{k % 2}")
            ts("vector", sp1[:], dtraw[:], dtb[:, 0:1], None, ALU.add, None,
               [kT("dtraw"), kT("dtb")], [kT("sp1")], tag=f"g{g}_dtz")
            ts("vector", dtraw[:], sp1[:], 0.0, None, ALU.max, None,
               [kT("sp1")], [kT("dtraw")], tag=f"g{g}_relu")
            act(dtv[:], sp1[:], AF.Abs, [kT("sp1")], [kT("dtv")], tag=f"g{g}_abs")
            act(dtv[:], dtv[:], AF.Exp, [kT("dtv")], [kT("dtv")], scale=-1.0, tag=f"g{g}_exp")
            act(dtv[:], dtv[:], AF.Ln, [kT("dtv"), kT("constc")], [kT("dtv")], bias=constc[0:NH, 1:2], tag=f"g{g}_ln")
            tt("vector", dtv[:], dtv[:], dtraw[:], ALU.add, [kT("dtv"), kT("dtraw")], [kT("dtv")],
               tag=f"g{g}_sp")
            ts("vector", abt[:], dtv[:], expA[:, 0:1], -1.0, ALU.mult, ALU.mult,
               [kT("dtv"), kT("expA")], [kT("abt")], tag=f"g{g}_abt")
            if debug and g == 0:
                dma("sync", dbg["d_dt"], dtv[:], [kT("dtv")], [kT("dbg2")], tag="ddt",
                    force_inc=True, stream="dbg")

            for cc in range(NCH):
                ccol = cc * LC
                first_chunk = (g == 0 and cc == 0)

                for lb in range(2):
                    transp(pb2[:, lb * 64:(lb + 1) * 64],
                           abt[:, ccol + lb * 128: ccol + (lb + 1) * 128],
                           eye32[0:64, 0:64], [kT("abt")], kP(2, lb * 64, (lb + 1) * 64), tag=f"acT{lb}")
                    act(acT[:, lb * 64:(lb + 1) * 64], pb2[:, lb * 64:(lb + 1) * 64], AF.Copy,
                        [kP(2, lb * 64, (lb + 1) * 64)], [kT("acT", lb * 64, (lb + 1) * 64)],
                        tag=f"acTe{lb}")
                mm(pb3[:, 384:448], [(utri[:], acT[:, 0:64])],
                   [kT("utri"), kT("acT", 0, 64)], kP(3, 384, 448), tag="cums0")
                mm(pb3[:, 448:512], [(ones128[:], acT[:, 0:64]), (utri[:], acT[:, 64:128])],
                   [kT("ones128"), kT("acT", 0, 128)], kP(3, 448, 512), tag="cums1")
                for lb in range(2):
                    pslot = pb3[:, 384 + lb * 64: 448 + lb * 64]
                    kps = kP(3, 384 + lb * 64, 448 + lb * 64)
                    act(acumT[:, lb * 64:(lb + 1) * 64], pslot, AF.Copy, [kps],
                        [kT("acumT", lb * 64, (lb + 1) * 64)], tag=f"acumTe{lb}")
                    act(sdT[:, lb * 64:(lb + 1) * 64], pslot, AF.Exp, [kps],
                        [kT("sdT", lb * 64, (lb + 1) * 64)], tag=f"sdT{lb}")
                    ts("vector", nacT[:, lb * 64:(lb + 1) * 64], acumT[:, lb * 64:(lb + 1) * 64],
                       -1.0, None, ALU.mult, None, [kT("acumT", lb * 64, (lb + 1) * 64)],
                       [kT("nacT", lb * 64, (lb + 1) * 64)], tag=f"nacT{lb}")
                    transp(pb2[0:64, lb * 128:(lb + 1) * 128], acumT[:, lb * 64:(lb + 1) * 64],
                           eye32[:], [kT("acumT", lb * 64, (lb + 1) * 64)],
                           kP(2, lb * 128, (lb + 1) * 128), tag=f"afm{lb}")
                    act(acum[:, ccol + lb * 128: ccol + (lb + 1) * 128],
                        pb2[0:64, lb * 128:(lb + 1) * 128], AF.Copy,
                        [kP(2, lb * 128, (lb + 1) * 128)], [kT("acum", ccol, ccol + 256)],
                        tag=f"afme{lb}")
                cp("vector", asp1[:], acum[:, ccol:ccol + 256],
                   [kT("acum", ccol, ccol + 256)], [kasp1()], tag="as1")
                stt(aspr[:], asp1[:], -1.0, acum[:, ccol:ccol + 256], ALU.mult, ALU.add,
                    [kasp1(), kT("acum", ccol, ccol + 256)], [kaspr()], tag="as2")
                cp("vector", asp2[:], aspr[:], [kaspr()], [kasp2()], tag="as3")
                stt(aspr[:], asp2[:], -1.0, aspr[:], ALU.mult, ALU.add,
                    [kasp2(), kaspr()], [kaspr()], tag="as4")
                cp("vector", asp3[:], aspr[:], [kaspr()], [kasp3()], tag="as5")
                ts("vector", decf[:], acum[:, ccol:ccol + 256], -1.0,
                   acum[:, ccol + 255:ccol + 256], ALU.mult, ALU.add,
                   [kT("acum", ccol, ccol + 256)], [kT("decf")], tag="decf")
                act(decf2[:], decf[:], AF.Exp, [kT("decf")], [kT("decf2")], tag="decf2")
                tt("vector", ddf[:], decf2[:], dtv[:, ccol:ccol + 256], ALU.mult,
                   [kT("decf2"), kT("dtv")], [kT("ddf")], tag="ddf")
                for lb in range(2):
                    transp(pb2[:, lb * 64:(lb + 1) * 64], ddf[:, lb * 128:(lb + 1) * 128],
                           eye32[0:64, 0:64], [kT("ddf")], kP(2, lb * 64, (lb + 1) * 64), tag=f"ddT{lb}")
                    act(ddT[:, lb * 64:(lb + 1) * 64], pb2[:, lb * 64:(lb + 1) * 64], AF.Copy,
                        [kP(2, lb * 64, (lb + 1) * 64)], [kT("ddT", lb * 64, (lb + 1) * 64)],
                        tag=f"ddTe{lb}")
                    transp(pb2[:, 256 + lb * 64: 256 + (lb + 1) * 64],
                           dtv[:, ccol + lb * 128: ccol + (lb + 1) * 128], eye32[0:64, 0:64],
                           [kT("dtv")], kP(2, 256 + lb * 64, 256 + (lb + 1) * 64), tag=f"dtT{lb}")
                    act(dtT[:, lb * 64:(lb + 1) * 64], pb2[:, 256 + lb * 64:256 + (lb + 1) * 64],
                        AF.Ln, [kP(2, 256 + lb * 64, 256 + (lb + 1) * 64)],
                        [kT("dtT", lb * 64, (lb + 1) * 64)], tag=f"dtTe{lb}")
                if debug and g == 0:
                    dma("sync", dbg["d_acum"][:, ccol:ccol + 256], acum[:, ccol:ccol + 256],
                        [kT("acum", ccol, ccol + 256)], [kT("dbg3")], tag=f"dacum{cc}",
                        force_inc=True, stream="dbg")

                for lb in range(2):
                    lcol = ccol + lb * 128
                    for fbb in range(4):
                        f0 = fbb * 8
                        tr_batch8(
                            [XBC[f0 + j][:, lcol:lcol + 128] for j in range(8)],
                            ar3(kXt[lb][1] + f0 * 256, 256, 8, 128),
                            [kXt[lb]], [kXBC[f0 + j] for j in range(8)],
                            fbb % 2, tag=f"xt{lb}_{fbb}")
                    pslot = pb2.bitcast(BF16)[:, 0:128]
                    transp(pslot, XBC[32][:, lcol:lcol + 128], eye_bf[:],
                           [kXBC[32]], kP(2, 0, 64), tag=f"bt{lb}")
                    act(Bt[:, lb * 128:(lb + 1) * 128], pslot, AF.Copy, [kP(2, 0, 64)],
                        [kBt(lb * 128, (lb + 1) * 128)], tag=f"bte{lb}")

                mm(pb3[:, 0:256], [(XBC[32][:, ccol:ccol + 128], XBC[33][:, ccol:ccol + 256])],
                   [kXBC[32], kXBC[33]], kP(3, 0, 256), tag="gt0")
                mm(pb3[:, 256:384],
                   [(XBC[32][:, ccol + 128:ccol + 256], XBC[33][:, ccol + 128:ccol + 256])],
                   [kXBC[32], kXBC[33]], kP(3, 256, 384), tag="gt1")
                tt("vector", Gt[:, 0:256], pb3[:, 0:256], trim[:, 0:256], ALU.mult,
                   [kP(3, 0, 256), kT("trim")], [kGt(0, 256)], tag="gtm0")
                tt("vector", Gt[:, 256:384], pb3[:, 256:384], trim[:, 0:128], ALU.mult,
                   [kP(3, 256, 384), kT("trim")], [kGt(256, 384)], tag="gtm1")

                def stA(h):
                    hg, buf = h // 8, (h // 8) % 2
                    if h % 8 == 0:
                        for ci, aspx in enumerate((asp1, asp2, asp3)):
                            dma("gpsimd", AROW2[buf][ci:ci + 1, :],
                                aspx[hg * 8:(hg + 1) * 8, :],
                                [[kasp1(), kasp2(), kasp3()][ci]], [kAROW2[buf]],
                                tag=f"flat{hg}_{ci}", stream=f"flat{buf}")
                        dma("gpsimd", alastr[0:1, hg * 8:(hg + 1) * 8],
                            acum[hg * 8:(hg + 1) * 8, ccol + 255:ccol + 256],
                            [kT("acum", ccol, ccol + 256)], [kT("alastr", hg * 8, hg * 8 + 8)],
                            tag=f"alast{hg}", stream=f"flat{buf}")
                        mm(pb2[:, 384 + hg * 8:392 + hg * 8],
                           [(ones32[:], alastr[0:1, hg * 8:(hg + 1) * 8])],
                           [kT("ones32"), kT("alastr", hg * 8, hg * 8 + 8)],
                           kP(2), tag=f"etot{hg}")
                        act(exptot[:, hg * 8:(hg + 1) * 8], pb2[:, 384 + hg * 8:392 + hg * 8],
                            AF.Exp, [kP(2)],
                            [kT("exptot", hg * 8, hg * 8 + 8)], tag=f"etote{hg}")
                    sl = h % 2
                    pbR = pb4 if sl == 0 else pb0
                    kR = kP(4) if sl == 0 else kP(0)
                    mm(pbR[:, 0:256],
                       [(ones3bf[:], AROW2[buf][0:3, (h % 8) * 256:(h % 8 + 1) * 256])],
                       [kT("ones3"), kAROW2[buf]], kR, tag=f"R{h}")

                def stB1(h):
                    sl = h % 2
                    pbR = pb4 if sl == 0 else pb0
                    kR = kP(4) if sl == 0 else kP(0)
                    pE = preE[:, sl * 384: sl * 384 + 384]
                    kpE = kpreE(sl * 384, sl * 384 + 384)
                    ts("vector", pE[:, 0:256], pbR[:, 0:256],
                       nacT[:, h:h + 1], 0.0, ALU.add, ALU.min,
                       [kR, kT("nacT", 0, 64)], [kpE], tag=f"pE0_{h}")
                    ts("vector", pE[:, 256:384], pbR[:, 128:256],
                       nacT[:, 64 + h:65 + h], 0.0, ALU.add, ALU.min,
                       [kR, kT("nacT", 64, 128)], [kpE], tag=f"pE1_{h}")

                def stB2(h):
                    sl = h % 2
                    pE = preE[:, sl * 384: sl * 384 + 384]
                    kpE = kpreE(sl * 384, sl * 384 + 384)
                    eE = Et[:, sl * 384: sl * 384 + 384]
                    keE = kEt(sl * 384, sl * 384 + 384)
                    act(eE[:, 0:256], pE[:, 0:256], AF.Exp, [kpE, kT("dtT", 0, 64)], [keE],
                        bias=dtT[:, h:h + 1], tag=f"E0_{h}")
                    act(eE[:, 256:384], pE[:, 256:384], AF.Exp, [kpE, kT("dtT", 64, 128)], [keE],
                        bias=dtT[:, 64 + h:65 + h], tag=f"E1_{h}")

                def stB3(h):
                    sl = h % 2
                    eE = Et[:, sl * 384: sl * 384 + 384]
                    keE = kEt(sl * 384, sl * 384 + 384)
                    mt = Mt[:, sl * 384: sl * 384 + 384]
                    kmt = kMt(sl * 384, sl * 384 + 384)
                    tt("vector", mt[:, 0:384], Gt[:, 0:384], eE[:, 0:384], ALU.mult,
                       [kGt(0, 384), keE], [kmt], tag=f"Mt_{h}")
                    for lb in range(2):
                        act(xd2[:, sl * 128 + lb * 64: sl * 128 + (lb + 1) * 64],
                            Xt[lb][:, h * 64:(h + 1) * 64], AF.Copy,
                            [kXt[lb], kT("ddT", lb * 64, (lb + 1) * 64)],
                            [kxd2(sl * 128 + lb * 64, sl * 128 + (lb + 1) * 64)],
                            scale=ddT[:, lb * 64 + h:lb * 64 + h + 1],
                            tag=f"xd2_{h}_{lb}")

                def stC(h):
                    sl = h % 2
                    pbY = pb5 if sl == 0 else pb1
                    kYd = kP(5) if sl == 0 else kP(1)
                    pbO = pb6 if sl == 0 else pb3
                    kOf = kP(6) if sl == 0 else kP(3)
                    pbS = pb7 if sl == 0 else pb2
                    kSt = kP(7) if sl == 0 else kP(2)
                    mt = Mt[:, sl * 384: sl * 384 + 384]
                    kmt = kMt(sl * 384, sl * 384 + 384)
                    mm(pbY[:, 0:64],
                       [(mt[:, 0:128], Xt[0][:, h * 64:(h + 1) * 64])],
                       [kmt, kXt[0]], kYd, tag=f"yd0_{h}")
                    mm(pbY[:, 64:128],
                       [(mt[:, 128:256], Xt[0][:, h * 64:(h + 1) * 64]),
                        (mt[:, 256:384], Xt[1][:, h * 64:(h + 1) * 64])],
                       [kmt, kXt[0], kXt[1]], kYd, tag=f"yd1_{h}")
                    if not first_chunk:
                        mm(pbO[:, 0:64],
                           [(XBC[33][:, ccol:ccol + 128], carry[:, h * 64:(h + 1) * 64])],
                           [kXBC[33], kT("carry", h * 64, (h + 1) * 64)],
                           kOf, tag=f"yo0_{h}")
                        mm(pbO[:, 64:128],
                           [(XBC[33][:, ccol + 128:ccol + 256], carry[:, h * 64:(h + 1) * 64])],
                           [kXBC[33], kT("carry", h * 64, (h + 1) * 64)],
                           kOf, tag=f"yo1_{h}")
                    mm(pbS[:, 0:64],
                       [(Bt[:, 0:128], xd2[:, sl * 128: sl * 128 + 64]),
                        (Bt[:, 128:256], xd2[:, sl * 128 + 64: sl * 128 + 128])],
                       [kBt(0, 256), kxd2(sl * 128, sl * 128 + 128)],
                       kSt, tag=f"st_{h}")

                def stD(h):
                    sl = h % 2
                    pbY = pb5 if sl == 0 else pb1
                    kYd = kP(5) if sl == 0 else kP(1)
                    pbO = pb6 if sl == 0 else pb3
                    kOf = kP(6) if sl == 0 else kP(3)
                    pbS = pb7 if sl == 0 else pb2
                    kSt = kP(7) if sl == 0 else kP(2)
                    yv3 = AR[0:128, kY[2 * cc][1] + h * 128: kY[2 * cc][1] + h * 128 + 128]\
                        .bitcast(BF16).rearrange("p (lb o) -> p lb o", lb=1)
                    yv = bass.AP(
                        tensor=yv3.tensor, offset=yv3.offset,
                        ap=[[yv3.ap[0][0], 128], [4096, 2], [1, 64]])
                    xv3 = AR[0:128, kXt[0][1] + h * 128: kXt[0][1] + h * 128 + 128]\
                        .bitcast(BF16)
                    xv = bass.AP(
                        tensor=xv3.tensor, offset=xv3.offset,
                        ap=[[xv3.ap[0][0], 128], [4096, 2], [1, 64]])
                    kysf = ("AR", kY[2 * cc][1] + h * 128, kY[2 * cc + 1][1] + (h + 1) * 128)
                    stt(yv, xv, drep[:, h:h + 1],
                        pbY[:, 0:128].rearrange("p (a b) -> p a b", a=2),
                        ALU.mult, ALU.add,
                        [kXt[0], kXt[1], kT("drep"), kYd], [kysf], tag=f"ev1_{h}")
                    for lb in range(2):
                        ysl = Yt[2 * cc + lb][:, h * 64:(h + 1) * 64]
                        kys = ("AR", kY[2 * cc + lb][1] + h * 128,
                               kY[2 * cc + lb][1] + (h + 1) * 128)
                        if not first_chunk:
                            stt(ysl, pbO[:, lb * 64:(lb + 1) * 64],
                                sdT[:, lb * 64 + h:lb * 64 + h + 1], ysl,
                                ALU.mult, ALU.add,
                                [kOf, kT("sdT", lb * 64, (lb + 1) * 64), kys],
                                [kys], tag=f"ev2_{h}_{lb}")
                    stt(carry[:, h * 64:(h + 1) * 64], carry[:, h * 64:(h + 1) * 64],
                        exptot[:, h:h + 1], pbS[:, 0:64],
                        ALU.mult, ALU.add,
                        [kT("carry", h * 64, (h + 1) * 64), kT("exptot", h, h + 1), kSt],
                        [kT("carry", h * 64, (h + 1) * 64)], tag=f"cu_{h}")

                for it in range(NH + 5):
                    if it < NH:
                        stA(it)
                    if 1 <= it < NH + 1:
                        stB1(it - 1)
                    if 2 <= it < NH + 2:
                        stB2(it - 2)
                    if 3 <= it < NH + 3:
                        stB3(it - 3)
                    if 4 <= it < NH + 4:
                        stC(it - 4)
                    if 5 <= it < NH + 5:
                        stD(it - 5)

            if debug and g == 0:
                for i in range(4):
                    dma("gpsimd", dbg["d_y"][i * 128:(i + 1) * 128, :], Yt[i][:, :],
                        [kY[i]], [kT("dbg5")], tag=f"dy{i}", force_inc=True, stream="dbg")

            for i in range(NTT):
                xi, kxi = (xn1, kxn1()) if i % 2 == 0 else (xn1b, kAR(86, 4096))
                ts("vector", xi[:], xres_t[i], statc[:, 8 + i:9 + i], statc[:, 12 + i:13 + i],
                   ALU.add, ALU.mult,
                   [kxres[i], kT("statc", 8 + i, 9 + i), kT("statc", 12 + i, 13 + i)],
                   [kxi], tag=f"g{g}_ln1b{i}")
                for fbb in range(2):
                    f0 = fbb * 8
                    tr_batch8(
                        [xi[:, (f0 + j) * 128:(f0 + j + 1) * 128] for j in range(8)],
                        ar3(kXNT[f0][1] + i * 256, 1024, 8, 128),
                        [("AR", kXNT[f0 + j][1] + i * 256,
                          kXNT[f0 + j][1] + (i + 1) * 256) for j in range(8)],
                        [kxi] * 8, (i * 2 + fbb) % 2, tag=f"g{g}_xnU{i}_{fbb}")
            def rms_fin(i):
                act(statc[:, 16 + i:17 + i], statc[:, 16 + i:17 + i], AF.Sqrt,
                    [kT("statc", 16 + i, 17 + i), kT("constc")], [kT("statc", 16 + i, 17 + i)],
                    bias=constc[:, 0:1], scale=1.0 / GATE, tag=f"g{g}_rms_d{i}")
                P.emit("vector", lambda v, o=statc[:, 16 + i:17 + i]: v.reciprocal(out=o, in_=o),
                       reads=[kT("statc", 16 + i, 17 + i)], writes=[kT("statc", 16 + i, 17 + i)],
                       tag=f"g{g}_rms_e{i}")
                stt(Yt[i][:, :], Yt[i][:, :], statc[:, 16 + i:17 + i], nwrep[:, :],
                    ALU.mult, ALU.mult,
                    [kY[i], kT("statc", 16 + i, 17 + i), kT("nwrep", 0, GATE)], [kY[i]],
                    tag=f"g{g}_rms_f{i}")
            for ob in range(8):
                wg = WGATE2[ob % 2]
                kwg = kWGATE2[ob % 2]
                for h in range(2):
                    dma("sync",
                        wg[:, h * 4096:(h + 1) * 4096].rearrange("p (k o) -> p k o", o=512),
                        scr_win[h * 8:(h + 1) * 8, :, ob * 512:(ob + 1) * 512]
                        .rearrange("k p o -> p k o"),
                        [kT("scre", h * 8, (h + 1) * 8)],
                        [("AR", kwg[1] + h * 8192, kwg[1] + (h + 1) * 8192)],
                        tag=f"g{g}_wg{ob}_{h}",
                        stream=f"wg{ob % 2}{'b' if h else ''}")
                for i in range(NTT):
                    pb = pb4r[i]
                    pairs = [(XNT[k][:, i * 128:(i + 1) * 128], wg[:, k * 512:(k + 1) * 512])
                             for k in range(16)]
                    pairs.append((ones_bf[0:1, 0:128], b_bf[0:1, ob * 512:(ob + 1) * 512]))
                    mm(pb[:, :], pairs,
                       [kwg] + [("AR", kXNT[k][1] + i * 256, kXNT[k][1] + (i + 1) * 256)
                                for k in range(16)]
                       + [kT("onesb"), kT("b_bf")], kP4(i), tag=f"g{g}_gate{ob}_{i}")
                    gs, kgs = (gsb[:, :], kT("gsb")) if (ob * 4 + i) % 2 == 0 \
                        else (sqd[:, 512:1024], ("sqd", 512, 1024))
                    act(gs, pb[:, :], AF.Silu, [kP4(i)], [kgs],
                        tag=f"g{g}_gs{ob}_{i}")
                    ysl = Yt[i][:, ob * 512:(ob + 1) * 512]
                    kys = ("AR", kY[i][1] + ob * 1024, kY[i][1] + (ob + 1) * 1024)
                    tt("vector", ysl, ysl, gs, ALU.mult, [kys, kgs], [kys],
                       tag=f"g{g}_xg{ob}_{i}")
                    sc = (16 + i) if ob == 0 else (20 + i) if ob % 2 == 0 else (80 + i)
                    act(sqd[:, 0:512], ysl, AF.Square, [kys],
                        [("sqd", 0, 512), kT("statc", sc, sc + 1)],
                        accum_out=statc[:, sc:sc + 1], tag=f"g{g}_rms_a{i}_{ob}")
                    if ob > 0:
                        tt("vector", statc[:, 16 + i:17 + i], statc[:, 16 + i:17 + i],
                           statc[:, sc:sc + 1], ALU.add,
                           [kT("statc", 16 + i, 17 + i), kT("statc", sc, sc + 1)],
                           [kT("statc", 16 + i, 17 + i)], tag=f"g{g}_rms_c{i}_{ob}")
                    if ob == 7:
                        rms_fin(i)
            for i in range(NTT):
                for fbb in range(4):
                    f0 = fbb * 8
                    tr_batch8(
                        [Yt[i][:, (f0 + j) * 128:(f0 + j + 1) * 128] for j in range(8)],
                        ar3(kXGT[i][1] + f0 * 256, 256, 8, 128),
                        [kXGT[i]], [kY[i]] * 8, fbb % 2, tag=f"g{g}_xgt{i}_{fbb}")
            for h in range(2):
                dma("sync",
                    WFFN2[0][:, h * 4096:(h + 1) * 4096].rearrange("p (k o) -> p k o", o=512),
                    wdt_scr[0][h * 8:(h + 1) * 8].rearrange("k p o -> p k o"),
                    [kT("wdtscr", h * 8, (h + 1) * 8)],
                    [("AR", kWFFN2[0][1] + h * 8192, kWFFN2[0][1] + (h + 1) * 8192)],
                    tag=f"g{g}_fw0pre{h}", stream=f"wf0{'b' if h else ''}")
            for oq in range(4):
                wo = WOUT2[oq % 2]
                kwo = kWOUT2[oq % 2]
                for qh in range(4):
                    dma("sync",
                        wo[:, qh * 4096:(qh + 1) * 4096].rearrange("p (k o) -> p k o", o=512),
                        scr_wo[qh * 8:(qh + 1) * 8, :, oq * 512:(oq + 1) * 512]
                        .rearrange("k p o -> p k o"),
                        [kT("scro", qh * 8, (qh + 1) * 8)],
                        [("AR", kwo[1] + qh * 8192, kwo[1] + (qh + 1) * 8192)],
                        tag=f"g{g}_wo{oq}_{qh}",
                        stream=f"wo{oq % 2}{['', 'b', 'c', 'd'][qh]}")
                for i in range(NTT):
                    pb = pb4r[i]
                    pairs = [(XGT[i][:, fb * 128:(fb + 1) * 128], wo[:, fb * 512:(fb + 1) * 512])
                             for fb in range(32)]
                    pairs.append((ones_bf[0:1, 0:128], outb_bf[0:1, oq * 512:(oq + 1) * 512]))
                    mm(pb[:, :], pairs,
                       [kXGT[i], kT("onesb"), kT("outb"), kwo],
                       kP4(i), tag=f"g{g}_op{i}_{oq}")
                    tt("vector", xres_t[i][:, oq * 512:(oq + 1) * 512],
                       xres_t[i][:, oq * 512:(oq + 1) * 512], pb[:, :], ALU.add,
                       [kT("xres", i * D + oq * 512, i * D + (oq + 1) * 512), kP4(i)],
                       [kT("xres", i * D + oq * 512, i * D + (oq + 1) * 512)],
                       tag=f"g{g}_opev{i}_{oq}")
                    if oq == 3:
                        ln_stats(24 + i, 28 + i, 32 + i, 36 + i, xres_t[i], kxres[i],
                                 f"g{g}_ln2_{i}")

            if debug and g == 0:
                for i in range(4):
                    dma("sync", dbg["d_x1"][i * 128:(i + 1) * 128, :], xres_t[i],
                        [kxres[i]], [kT("dbg6")], tag=f"dx1_{i}", force_inc=True, stream="dbg")

            for i in range(NTT):
                X2, Q8v, XDQ, kX2, q8base, dqbase = (
                    (X2N, Q8, XDEQ, kX2N, 90 * 1024, 92 * 1024) if i % 2 == 0 else
                    (X2Nb, Q8b, XDEQb, kAR(50, 8192), 58 * 1024, 60 * 1024))
                ts("vector", X2[:, :], xres_t[i], statc[:, 32 + i:33 + i],
                   statc[:, 36 + i:37 + i], ALU.add, ALU.mult,
                   [kxres[i], kT("statc", 32 + i, 33 + i), kT("statc", 36 + i, 37 + i)],
                   [kX2], tag=f"g{g}_ln2ap{i}")
                for b in range(16):
                    vred(statc[:, 40 + b:41 + b], X2[:, b * 128:(b + 1) * 128], True,
                         [kT("statc", 40, 56)], [kX2], tag=f"g{g}_qmx{i}_{b}")
                ts("vector", statc[:, 40:56], statc[:, 40:56], 2.0 / 448.0, None, ALU.mult, None,
                   [kT("statc", 40, 56)], [kT("statc", 40, 56)], tag=f"g{g}_qs{i}")
                P.emit("vector", lambda v: v.reciprocal(out=statc[:, 64:80], in_=statc[:, 40:56]),
                       reads=[kT("statc", 40, 56)], writes=[kT("statc", 64, 80)], tag=f"g{g}_qsi{i}")
                for b in range(16):
                    kQ8b = ("AR", q8base + b * 128, q8base + (b + 1) * 128)
                    kXDQb = ("AR", dqbase + b * 256, dqbase + (b + 1) * 256)
                    ts("vector", Q8v[:, b * 128:(b + 1) * 128], X2[:, b * 128:(b + 1) * 128],
                       statc[:, 64 + b:65 + b], None, ALU.mult, None,
                       [kX2, kT("statc", 64, 80)], [kQ8b], tag=f"g{g}_q{i}_{b}")
                    act(XDQ[:, b * 128:(b + 1) * 128], Q8v[:, b * 128:(b + 1) * 128],
                        AF.Copy, [kQ8b, kT("statc", 40, 56)], [kXDQb],
                        scale=statc[:, 40 + b:41 + b], tag=f"g{g}_dq{i}_{b}")
                if debug and g == 0 and i == 0:
                    dma("sync", dbg["d_x2n"], X2N[:, :], [kX2N], [kT("dbg7")],
                        tag="dx2n", force_inc=True, stream="dbg")
                    dma("gpsimd", dbg["d_xdeq"], XDEQ[:, :], [kXDEQ], [kT("dbg8")],
                        tag="dxdeq", force_inc=True, stream="dbg")
                for kb in range(2):
                    k0 = kb * 8
                    tr_batch8(
                        [XDQ[:, (k0 + j) * 128:(k0 + j + 1) * 128] for j in range(8)],
                        ar3(kXDT[k0][1] + i * 256, 1024, 8, 128),
                        [("AR", kXDT[k0 + j][1] + i * 256,
                          kXDT[k0 + j][1] + (i + 1) * 256) for j in range(8)],
                        [("AR", dqbase + (k0 + j) * 256, dqbase + (k0 + j + 1) * 256)
                         for j in range(8)], (i * 2 + kb) % 2, tag=f"g{g}_fxt{i}_{kb}")
            for oh in range(4):
                wf = WFFN2[oh % 2]
                kwf = kWFFN2[oh % 2]
                if oh > 0:
                    for h in range(2):
                        dma("sync",
                            wf[:, h * 4096:(h + 1) * 4096].rearrange("p (k o) -> p k o", o=512),
                            wdt_scr[oh][h * 8:(h + 1) * 8].rearrange("k p o -> p k o"),
                            [kT("wdtscr", oh * 16 + h * 8, oh * 16 + (h + 1) * 8)],
                            [("AR", kwf[1] + h * 8192, kwf[1] + (h + 1) * 8192)],
                            tag=f"g{g}_fw{oh}_{h}", stream=f"wf{oh % 2}{'b' if h else ''}")
                if debug and g == 0 and oh == 0:
                    dma("gpsimd", dbg["d_wf"], WFFN2[0][:, 0:512], [kWFFN2[0]], [kT("dbg9")],
                        tag="dwf", force_inc=True, stream="dbg")
                for i in range(NTT):
                    pb = pb4r[i]
                    pairs = [(XDT[k][:, i * 128:(i + 1) * 128], wf[:, k * 512:(k + 1) * 512])
                             for k in range(16)]
                    mm(pb[:, :], pairs,
                       [kwf] + [("AR", kXDT[k][1] + i * 256, kXDT[k][1] + (i + 1) * 256)
                                for k in range(16)],
                       kP4(i), tag=f"g{g}_ffn{oh}_{i}")
                    tt("vector", xres_t[i][:, oh * 512:(oh + 1) * 512],
                       xres_t[i][:, oh * 512:(oh + 1) * 512], pb[:, :], ALU.add,
                       [kT("xres", i * D + oh * 512, i * D + (oh + 1) * 512), kP4(i)],
                       [kT("xres", i * D + oh * 512, i * D + (oh + 1) * 512)],
                       tag=f"g{g}_ffnev{i}_{oh}")
                    if oh == 3:
                        dma("sync", out_h[(g * NTT + i) * 128:(g * NTT + i + 1) * 128, :],
                            xres_t[i],
                            [kxres[i]], [kT("OUT", g * NTT + i, g * NTT + i + 1)],
                            tag=f"g{g}_st{i}", force_inc=True, stream=f"store{i}")
                        if g + 1 < NG:
                            dma("sync", xres_t[i],
                                x_in[((g + 1) * NTT + i) * 128:((g + 1) * NTT + i + 1) * 128, :],
                                [kxres[i]], [kxres[i]], tag=f"g{g + 1}_xld{i}",
                                stream=f"xload{i}")
                            ln1_tile(g + 1, i)

        P.finalize()
        assert len(P.streams) <= len(ssem_pool), P.streams
        ssems = {name: ssem_pool[i] for i, name in enumerate(P.streams)}
        final_waits = [(name, 16 * P.stream_count[name]) for name in P.streams
                       if name.startswith("store") or name == "dbg"]

        @block.sync
        def _(sync):
            P.run("sync", sync, sems, ssems)
            for name, v in final_waits:
                sync.wait_ge(ssems[name], v)

        @block.tensor
        def _(t):
            P.run("tensor", t, sems, ssems)

        @block.vector
        def _(v):
            P.run("vector", v, sems, ssems)

        @block.scalar
        def _(s):
            P.run("scalar", s, sems, ssems)

        @block.gpsimd
        def _(gp):
            P.run("gpsimd", gp, sems, ssems)

    return nc


_cached = {}


def _get_program(debug=False):
    key = bool(debug)
    if key not in _cached:
        _cached[key] = build_program(debug=key)
    return _cached[key]


def kernel(L0_attn_hidden_states, L0_attn_in_proj_w, L0_attn_in_proj_b,
           L0_attn_conv_w, L0_attn_dt_bias, L0_attn_A_log, L0_attn_D,
           L0_attn_norm_weight, L0_attn_out_proj_w, L0_attn_out_proj_b,
           L0_ffn_weight, _debug=False, _cores=None):
    hs = np.ascontiguousarray(np.asarray(L0_attn_hidden_states, dtype=np.float32))
    const = {
        "w_in": np.ascontiguousarray(np.asarray(L0_attn_in_proj_w, np.float32)),
        "b_in": np.ascontiguousarray(np.asarray(L0_attn_in_proj_b, np.float32).reshape(1, PROJ)),
        "conv_w": np.ascontiguousarray(np.asarray(L0_attn_conv_w, np.float32)),
        "dt_bias": np.ascontiguousarray(np.asarray(L0_attn_dt_bias, np.float32).reshape(NH, 1)),
        "a_log": np.ascontiguousarray(np.asarray(L0_attn_A_log, np.float32).reshape(NH, 1)),
        "d_head": np.ascontiguousarray(np.asarray(L0_attn_D, np.float32).reshape(1, NH)),
        "nw": np.ascontiguousarray(np.asarray(L0_attn_norm_weight, np.float32).reshape(1, GATE)),
        "w_out": np.ascontiguousarray(np.asarray(L0_attn_out_proj_w, np.float32)),
        "b_out": np.ascontiguousarray(np.asarray(L0_attn_out_proj_b, np.float32).reshape(1, D)),
        "w_ffn": np.ascontiguousarray(np.asarray(L0_ffn_weight, np.float32)),
        "eye_bf": np.eye(128, dtype=ml_dtypes.bfloat16),
        "eye32": np.eye(128, dtype=np.float32),
        "utri": np.triu(np.ones((128, 128), np.float32)),
        "trimask": np.concatenate(
            [np.triu(np.ones((128, 128), np.float32)),
             np.ones((128, 128), np.float32)], axis=1).astype(ml_dtypes.bfloat16),
    }
    ncores = hs.shape[0]
    nc = _get_program(debug=_debug)
    in_maps = [dict(const, x=np.ascontiguousarray(hs[i])) for i in range(ncores)]
    core_ids = list(range(ncores)) if _cores is None else _cores
    res = run_bass_kernel_spmd(nc, in_maps, core_ids=core_ids)
    out = np.stack([res.results[i]["out"] for i in range(len(core_ids))], axis=0)
    if _debug:
        return out, [res.results[i] for i in range(len(core_ids))]
    return out
